# version 16
# baseline (speedup 1.0000x reference)
import numpy as np
import concourse.bass as bass
import concourse.mybir as mybir
from concourse.bass_utils import run_bass_kernel_spmd
from contextlib import ExitStack

F32 = mybir.dt.float32
BF16 = mybir.dt.bfloat16
AF = mybir.ActivationFunctionType
ALU = mybir.AluOpType
AX = mybir.AxisListType

S = 4096
H = 1024
NH = 16
HD = 64
NE = 8
FF = 4096
HC = 8
SG = 512
NSG = S // SG
MG = 1024
NMG = S // MG
FS = 256
NFS = FF // FS
EPS = 1e-5
JEPS = 0.02

ENGS = ("tensor", "vector", "scalar", "gpsimd", "sync")


class Prog:

    def __init__(self):
        self.items = {e: [] for e in ENGS}
        self.ctr = {}

    def cnt(self, sem):
        return self.ctr.get(id(sem), 0)

    def snap(self, sems):
        return [(s, self.cnt(s)) for s in sems]

    def do(self, eng, fn, waits=(), inc=None, incn=1):
        ws = [(s, v) for (s, v) in waits if v > 0]
        if inc is not None and eng in ("vector", "scalar"):
            pv = self.cnt(inc)
            if pv > 0:
                ws.append((inc, pv))
        if inc is not None:
            self.ctr[id(inc)] = self.ctr.get(id(inc), 0) + incn
        self.items[eng].append((fn, ws, inc, incn))
        return self.ctr.get(id(inc), 0) if inc is not None else None

    def emit(self, nc, barrier_waits=None):
        bw = [(s, v) for (s, v) in (barrier_waits or []) if v > 0]
        with nc.Block() as block:
            for eng in ENGS:
                items = self.items[eng]
                if not items and not bw:
                    continue

                def make(items=items):
                    def run(e):
                        seen = {}
                        for (s, v) in bw:
                            e.wait_ge(s, v)
                            seen[id(s)] = v
                        for (fn, ws, inc, incn) in items:
                            for (s, v) in ws:
                                if seen.get(id(s), -1) < v:
                                    e.wait_ge(s, v)
                                    seen[id(s)] = v
                            ins = fn(e)
                            if inc is not None:
                                ins.then_inc(inc, incn)
                    return run

                getattr(block, eng)(make())
        self.items = {e: [] for e in ENGS}


def build(debug=False, n_phases=4):
    nc = bass.Bass("TRN2", target_bir_lowering=False)

    x_fm = nc.dram_tensor("x_fm", [H, S], F32, kind="ExternalInput").ap()
    wqkvP = nc.dram_tensor("wqkvP", [6, 128, HC, 512], F32, kind="ExternalInput").ap()
    woutP = nc.dram_tensor("woutP", [HC, 128, H], F32, kind="ExternalInput").ap()
    w13P = nc.dram_tensor("w13P", [NE, NFS, 128, 2, HC, FS], F32, kind="ExternalInput").ap()
    w2P = nc.dram_tensor("w2P", [NE, 32, 128, 1024], F32, kind="ExternalInput").ap()
    gateP = nc.dram_tensor("gateP", [HC, 128, NE], F32, kind="ExternalInput").ap()
    bqkvP = nc.dram_tensor("bqkvP", [128, 24], F32, kind="ExternalInput").ap()
    bvrow = nc.dram_tensor("bvrow", [1, H], F32, kind="ExternalInput").ap()
    boutP = nc.dram_tensor("boutP", [128, HC], F32, kind="ExternalInput").ap()
    gammaP = nc.dram_tensor("gammaP", [128, HC], F32, kind="ExternalInput").ap()
    betaP = nc.dram_tensor("betaP", [128, HC], F32, kind="ExternalInput").ap()
    identP = nc.dram_tensor("identP", [128, 128], F32, kind="ExternalInput").ap()
    onesP = nc.dram_tensor("onesP", [128, 4], F32, kind="ExternalInput").ap()
    onesrP = nc.dram_tensor("onesrP", [1, 128], F32, kind="ExternalInput").ap()
    selP = nc.dram_tensor("selP", [NE, NE * 128], F32, kind="ExternalInput").ap()
    jiotaP = nc.dram_tensor("jiotaP", [128, 384], F32, kind="ExternalInput").ap()
    piotaP = nc.dram_tensor("piotaP", [128, 1], F32, kind="ExternalInput").ap()

    w13B = nc.dram_tensor("w13B", [NE, NFS, 128, 2, HC, FS], BF16).ap()
    w2B = nc.dram_tensor("w2B", [NE, 32, 128, 1024], BF16).ap()
    sk = {"kind": "ExternalOutput"} if debug else {}
    xn_hbm = nc.dram_tensor("xn_hbm", [H, S], BF16, **sk).ap()
    q_hbm = nc.dram_tensor("q_hbm", [H, S], BF16, **sk).ap()
    s_hbm = nc.dram_tensor("s_hbm", [H, S], BF16, **sk).ap()
    g_hbm = nc.dram_tensor("g_hbm", [H, S], BF16, **sk).ap()
    y_hbm = nc.dram_tensor("y", [H, S], F32, kind="ExternalOutput").ap()
    if debug:
        k_dbg = nc.dram_tensor("k_dbg", [H, S], BF16, kind="ExternalOutput").ap()
        v_dbg = nc.dram_tensor("v_dbg", [128, 32, 1040], BF16, kind="ExternalOutput").ap()
        wte_dbg = nc.dram_tensor("wte_dbg", [NE, S], BF16, kind="ExternalOutput").ap()
        cb_dbg = nc.dram_tensor("cb_dbg", [128, NH], F32, kind="ExternalOutput").ap()

    P = Prog()
    FMRE = "(c p) s -> p c s"

    with ExitStack() as eso:
        sems = [eso.enter_context(nc.semaphore(n))
                for n in ("ldg", "lds", "smm", "sac", "sev")]
        ldg, lds, smm, sac, sev = sems

        wte_fm = eso.enter_context(nc.sbuf_tensor("wte_fm", [NE, S], BF16))
        ones_bf = eso.enter_context(nc.sbuf_tensor("ones_bf", [128, 4], BF16))
        onesr_bf = eso.enter_context(nc.sbuf_tensor("onesr_bf", [1, 128], BF16))
        cbias = eso.enter_context(nc.sbuf_tensor("cbias", [128, NH], F32))

        P.do("gpsimd", lambda e: e.dma_start(out=ones_bf[:, :], in_=onesP), inc=ldg, incn=16)
        P.do("gpsimd", lambda e: e.dma_start(out=onesr_bf[:, :], in_=onesrP), inc=ldg, incn=16)
        lds_consts = P.cnt(lds)
        ldg_consts = P.cnt(ldg)

        with ExitStack() as eskv:
            Tk = lambda *a: eskv.enter_context(nc.sbuf_tensor(*a))
            K_res = Tk("K_res", [128, HC, S], BF16)
            V_res = Tk("V_res", [128, 32, 1040], BF16)

            precast = []
            for _e in range(NE):
                for _fs in range(NFS):
                    precast.append((w13B[_e, _fs], w13P[_e, _fs]))
            for _e in range(NE):
                for _q4 in range(4):
                    precast.append((w2B[_e, _q4 * 8:(_q4 + 1) * 8],
                                    w2P[_e, _q4 * 8:(_q4 + 1) * 8]))
            precast_i = [0]

            def rec_precast(n):
                while n > 0 and precast_i[0] < len(precast):
                    dst, srcp = precast[precast_i[0]]
                    precast_i[0] += 1
                    P.do("gpsimd", lambda e, dst=dst, srcp=srcp: e.dma_start(
                        out=dst, in_=srcp), inc=ldg, incn=16)
                    n -= 1

            with ExitStack() as es1:
                T = lambda *a: es1.enter_context(nc.sbuf_tensor(*a))
                wqbuf = T("wqbuf", [128, 2, HC, 512], BF16)
                ident_sb = T("ident_sb", [128, 128], F32)
                xbf = T("xbf", [128, HC, SG], BF16)
                xf32 = T("xf32", [128, 2, SG], F32)
                sqb = T("sqb", [128, 2, SG], BF16)
                xn_grp = T("xn_grp", [128, HC, SG], BF16)
                xn2_grp = T("xn2_grp", [128, HC, SG], BF16)
                u_sb = T("u_sb", [128, SG], F32)
                AB_sb = T("AB_sb", [128, 2 * SG], F32)
                rowsA = T("rowsA", [1, SG * 3], F32)
                rowsB = T("rowsB", [1, SG * 2], BF16)
                qst = T("qst", [128, 2, SG], BF16)
                gate_sb = T("gate_sb", [128, HC, NE], BF16)
                bqkv_sb = T("bqkv_sb", [128, 24], F32)
                gm_sb = T("gm_sb", [128, HC], F32)
                bt_sb = T("bt_sb", [128, HC], F32)
                vb_sb = T("vb_sb", [128, H], BF16)
                bv_bf = T("bv_bf", [1, H], BF16)
                maxflat = T("maxflat", [1, 64], F32)
                rm_sb = T("rm_sb", [2, 1], F32)
                rt = {}
                for n in ("m1", "nm1", "sum1", "mult1", "m2", "nm2", "sum2", "mult2"):
                    rt[n] = T("rt_" + n, [128, 1], F32)
                for n in ("is1", "f1", "d1", "t1", "keep1", "ex1", "e1", "l2", "absl", "lgs",
                          "is2", "f2", "d2", "t2", "keep2", "ex2e", "e2", "wte_tm"):
                    rt[n] = T("rt_" + n, [128, NE], F32)

                with ExitStack() as ps1:
                    PT = lambda *a: ps1.enter_context(nc.psum_tensor(*a))
                    ps_stA = PT("ps_stA", [128, 512], F32)
                    ps_stB = PT("ps_stB", [128, 512], F32)
                    ps_ab = PT("ps_ab", [128, 1024], F32)
                    ps_rot = PT("ps_rot", [128, 1536], F32)
                    ps_rt = PT("ps_rt", [128, 512], F32)

                    P.do("gpsimd", lambda e: e.dma_start(out=gate_sb[:, :, :], in_=gateP.rearrange("c p e -> p c e")),
                         inc=ldg, incn=16)
                    P.do("gpsimd", lambda e: e.dma_start(out=bv_bf[:, :], in_=bvrow),
                         inc=ldg, incn=16)
                    ldg_small = P.cnt(ldg)
                    P.do("sync", lambda e: e.dma_start(out=ident_sb[:, :], in_=identP), inc=lds, incn=16)
                    P.do("sync", lambda e: e.dma_start(out=bqkv_sb[:, :], in_=bqkvP), inc=lds, incn=16)
                    P.do("sync", lambda e: e.dma_start(out=gm_sb[:, :], in_=gammaP), inc=lds, incn=16)
                    P.do("sync", lambda e: e.dma_start(out=bt_sb[:, :], in_=betaP), inc=lds, incn=16)
                    lds_small = P.cnt(lds)

                    P.do("vector", lambda e: e.memset(
                        V_res[:, :, :].rearrange("p k (h x) -> p k h x", x=65)[:, :, :, 64:65], 1.0),
                        inc=sev)
                    P.do("vector", lambda e: e.memset(maxflat[:, :], 0.0), inc=sev)

                    for half in range(2):
                        P.do("tensor", lambda e, h=half: e.matmul(
                            ps_ab[:, h * 512:(h + 1) * 512], onesr_bf[:, :],
                            bv_bf[:, h * 512:(h + 1) * 512], start=True, stop=True),
                            waits=[(ldg, max(ldg_small, ldg_consts))], inc=smm)
                    smm_vb = P.cnt(smm)
                    P.do("scalar", lambda e: e.activation(vb_sb[:, :], ps_ab[:, :], AF.Copy),
                         waits=[(smm, smm_vb)], inc=sac)
                    last_ab_reader = P.cnt(sac)
                    sac_vb = P.cnt(sac)

                    NSLAB = NSG * 6
                    ldg_slab = {}
                    smm_after_slab = {}

                    def rec_slab(k):
                        if k >= NSLAB:
                            return
                        j = k % 6
                        w = []
                        if k >= 2:
                            w.append((smm, smm_after_slab[k - 2]))
                        P.do("gpsimd", lambda e, j=j, k=k: e.dma_start(
                            out=wqbuf[:, k % 2], in_=wqkvP[j]), waits=w, inc=ldg, incn=16)
                        ldg_slab[k] = P.cnt(ldg)

                    rec_slab(0)
                    rec_slab(1)

                    ldg_xbf = {}
                    lds_xf = {}
                    lds_qstore = {}
                    sev_xn_done = {}
                    sev_xn2_done = {}
                    sev_evac_rot = {}
                    smm_norm = {}
                    sev_stab_end = {-1: 0}
                    sac_rt_end = {-1: 0}
                    sev_rt_end = {-1: 0}
                    sev_rt_last_of = {}
                    sac_wte = {}
                    sev_ab_read = 0
                    smm_stats_end = {-1: 0}
                    sac_sq_end = {-1: 0}
                    rot_i = [0]

                    stats_read2_of = {-1: 0}
                    smm_abcde_of = {}
                    smm_st2_of = {}
                    sac_B_end = {-1: 0}

                    def ln_rows(smm_done):
                        nonlocal last_ab_reader
                        mu = rowsA[:, 0:SG]
                        ex2 = rowsA[:, SG:2 * SG]
                        var = rowsA[:, 2 * SG:3 * SG]
                        Arow = rowsA[:, 2 * SG:3 * SG]
                        Brow = rowsA[:, 0:SG]
                        P.do("vector", lambda e: e.tensor_scalar(
                            mu, ps_stA[0:1, :], 1.0 / H, None, ALU.mult),
                            waits=[(smm, smm_done)], inc=sev)
                        P.do("vector", lambda e: e.tensor_scalar(
                            ex2, ps_stB[0:1, :], 1.0 / H, None, ALU.mult), inc=sev)
                        P.do("vector", lambda e: e.tensor_mul(var, mu, mu), inc=sev)
                        P.do("vector", lambda e: e.tensor_sub(var, ex2, var), inc=sev)
                        P.do("vector", lambda e: e.tensor_scalar(var, var, EPS, None, ALU.add),
                             inc=sev)
                        sev_v = P.cnt(sev)
                        stats_read = P.cnt(sev)
                        P.do("scalar", lambda e: e.activation(ex2, var, AF.Ln),
                             waits=[(sev, sev_v)], inc=sac)
                        P.do("scalar", lambda e: e.activation(Arow, ex2, AF.Exp, scale=-0.5),
                             inc=sac)
                        sac_a = P.cnt(sac)
                        P.do("vector", lambda e: e.tensor_mul(Brow, mu, Arow),
                             waits=[(sac, sac_a)], inc=sev)
                        P.do("vector", lambda e: e.tensor_copy(rowsB[:, 0:SG], Arow), inc=sev)
                        P.do("vector", lambda e: e.tensor_scalar(
                            rowsB[:, SG:2 * SG], Brow, -1.0, None, ALU.mult), inc=sev)
                        sev_rows = P.cnt(sev)
                        for h in range(2):
                            P.do("tensor", lambda e, h=h: e.matmul(
                                ps_ab[:, h * 512:(h + 1) * 512], onesr_bf[:, :],
                                rowsB[:, h * SG:(h + 1) * SG], start=True, stop=True),
                                waits=[(sev, sev_rows), (sac, last_ab_reader)], inc=smm)
                        smm_b = P.cnt(smm)
                        P.do("scalar", lambda e: e.activation(AB_sb[:, :], ps_ab[:, :], AF.Copy),
                             waits=[(smm, smm_b)], inc=sac)
                        last_ab_reader = P.cnt(sac)
                        return P.cnt(sac), stats_read


                    def recABCDE(g):
                        nonlocal sev_ab_read
                        smm_g0 = P.cnt(smm)
                        w = []
                        if g >= 1:
                            w.append((smm, smm_stats_end[g - 1]))
                            w.append((sac, sac_B_end[g - 1]))
                        P.do("gpsimd", lambda e, g=g: e.dma_start(
                            out=xbf[:, :, :],
                            in_=x_fm.rearrange(FMRE, p=128)[:, :, g * SG:(g + 1) * SG]),
                            waits=w, inc=ldg, incn=16)
                        ldg_xbf[g] = P.cnt(ldg)
                        def rec_xf(c, g=None):
                            gi = g * 8 + c
                            w = []
                            if gi >= 2:
                                pg, pc = divmod(gi - 2, 8)
                                w.append((sev, sev_xn_done[(pg, pc)]))
                            P.do("sync", lambda e, g=g, c=c: e.dma_start(
                                out=xf32[:, c % 2, :],
                                in_=x_fm[c * 128:(c + 1) * 128, g * SG:(g + 1) * SG]),
                                waits=w, inc=lds, incn=16)
                            lds_xf[(g, c)] = P.cnt(lds)

                        rec_xf(0, g)
                        rec_xf(1, g)

                        smm_sq_mm = {}
                        for c in range(HC):
                            w = [(ldg, ldg_xbf[g])]
                            if c >= 2:
                                w.append((smm, smm_sq_mm[c - 2]))
                            elif g >= 1:
                                w.append((smm, smm_g0))
                            P.do("scalar", lambda e, c=c: e.activation(
                                sqb[:, c % 2, :], xbf[:, c, :], AF.Square), waits=w, inc=sac)
                            sac_sq = P.cnt(sac)
                            w = [(ldg, max(ldg_xbf[g], ldg_consts))]
                            if c == 0 and g >= 1:
                                w.append((sev, stats_read2_of[g - 1]))
                            P.do("tensor", lambda e, c=c: e.matmul(
                                ps_stA[0:1, :], ones_bf[:, 0:1], xbf[:, c, :],
                                start=(c == 0), stop=(c == 7)), waits=w, inc=smm)
                            P.do("tensor", lambda e, c=c: e.matmul(
                                ps_stB[0:1, :], ones_bf[:, 0:1], sqb[:, c % 2, :],
                                start=(c == 0), stop=(c == 7)),
                                waits=[(sac, sac_sq)], inc=smm)
                            smm_sq_mm[c] = P.cnt(smm)
                        smm_st1 = P.cnt(smm)
                        sac_ab, sev_stats_read1 = ln_rows(smm_st1)

                        sac_B_end[g] = P.cnt(sac)
                        smm_sq2_mm = {}
                        for c in range(HC):
                            P.do("vector", lambda e, c=c: e.tensor_mul(
                                u_sb[:, :], xf32[:, c % 2, :], AB_sb[:, 0:SG]),
                                waits=[(lds, lds_xf[(g, c)]), (sac, sac_ab)], inc=sev)
                            P.do("vector", lambda e, c=c: e.tensor_add(
                                xn_grp[:, c, :], u_sb[:, :], AB_sb[:, SG:2 * SG]), inc=sev)
                            sev_xn_done[(g, c)] = P.cnt(sev)
                            if c + 2 < HC:
                                rec_xf(c + 2, g)
                            w = [(sev, sev_xn_done[(g, c)])]
                            if c < 2:
                                w.append((smm, smm_st1))
                            else:
                                w.append((smm, smm_sq2_mm[c - 2]))
                            P.do("scalar", lambda e, c=c: e.activation(
                                sqb[:, c % 2, :], xn_grp[:, c, :], AF.Square), waits=w, inc=sac)
                            sac_sq2 = P.cnt(sac)
                            w = [(sev, sev_xn_done[(g, c)])]
                            if c == 0:
                                w.append((sev, sev_stats_read1))
                            P.do("tensor", lambda e, c=c: e.matmul(
                                ps_stA[0:1, :], ones_bf[:, 0:1], xn_grp[:, c, :],
                                start=(c == 0), stop=(c == 7)), waits=w, inc=smm)
                            P.do("tensor", lambda e, c=c: e.matmul(
                                ps_stB[0:1, :], ones_bf[:, 0:1], sqb[:, c % 2, :],
                                start=(c == 0), stop=(c == 7)), waits=[(sac, sac_sq2)], inc=smm)
                            smm_sq2_mm[c] = P.cnt(smm)
                        smm_st2 = P.cnt(smm)
                        smm_stats_end[g] = P.cnt(smm)
                        P.do("sync", lambda e, g=g: e.dma_start(
                            out=xn_hbm.rearrange(FMRE, p=128)[:, :, g * SG:(g + 1) * SG],
                            in_=xn_grp[:, :, :]),
                            waits=[(sev, sev_xn_done[(g, 7)])], inc=lds, incn=16)

                        sac_ab2, sev_stats_read2 = ln_rows(smm_st2)
                        sac_ab2_of[g] = sac_ab2
                        stats_read2_of[g] = sev_stats_read2
                        smm_st2_of[g] = smm_st2
                        smm_abcde_of[g] = P.cnt(smm)

                    def recF(g):
                        nonlocal sev_ab_read
                        sac_ab2 = sac_ab2_of[g]
                        for c in range(HC):
                            P.do("vector", lambda e, c=c: e.tensor_mul(
                                u_sb[:, :], xn_grp[:, c, :], AB_sb[:, 0:SG]),
                                waits=[(sac, sac_ab2)], inc=sev)
                            P.do("vector", lambda e, c=c: e.tensor_add(
                                u_sb[:, :], u_sb[:, :], AB_sb[:, SG:2 * SG]), inc=sev)
                            P.do("vector", lambda e, c=c: e.scalar_tensor_tensor(
                                xn2_grp[:, c, :], u_sb[:, :], gm_sb[:, c:c + 1],
                                bass.AP(bt_sb, c, [[HC, 128], [0, SG]]),
                                ALU.mult, ALU.add),
                                waits=[(lds, lds_small)], inc=sev)
                            sev_xn2_done[(g, c)] = P.cnt(sev)
                        sev_ab_read = P.cnt(sev)


                    def recG(g):
                        nonlocal sev_rt_prev
                        for st in range(4):
                            w = [(sev, sev_xn_done[(g, 7)]), (ldg, ldg_small),
                                 (sev, sev_stab_end[g - 1]),
                                 (sac, sac_rt_end[g - 1] if st == 0 else sac_wte[(g, st - 1)]),
                                 (sev, sev_rt_end[g - 1] if st == 0 else 0)]
                            if st > 0:
                                w.append((sev, sev_rt_prev))
                            for c in range(HC):
                                P.do("tensor", lambda e, c=c, st=st: e.matmul(
                                    ps_rt[:, 0:NE], xn_grp[:, c, st * 128:(st + 1) * 128],
                                    gate_sb[:, c, :], start=(c == 0), stop=(c == 7)),
                                    waits=(w if c == 0 else []), inc=smm)
                            smm_lg = P.cnt(smm)
                            m1v = bass.AP(rt["m1"], 0, [[1, 128], [0, NE]])
                            m2v = bass.AP(rt["m2"], 0, [[1, 128], [0, NE]])
                            m1s = bass.AP(rt["mult1"], 0, [[1, 128], [0, NE]])
                            m2s = bass.AP(rt["mult2"], 0, [[1, 128], [0, NE]])

                            def V_(fn, ww=()):
                                P.do("vector", fn, waits=ww, inc=sev)

                            P.do("scalar", lambda e: e.activation(
                                rt["lgs"][:, :], ps_rt[:, 0:NE], AF.Copy),
                                waits=[(smm, smm_lg)], inc=sac)
                            P.do("scalar", lambda e: e.activation(
                                rt["absl"][:, :], ps_rt[:, 0:NE], AF.Abs),
                                waits=[(smm, smm_lg)], inc=sac)
                            sac_abs = P.cnt(sac)
                            V_(lambda e: e.reduce_max(rt["m1"][:, :], rt["lgs"][:, :], axis=AX.X),
                               [(sac, sac_abs)])
                            V_(lambda e: e.tensor_tensor(rt["is1"][:, :], rt["lgs"][:, :], m1v, ALU.is_equal))
                            V_(lambda e: e.tensor_tensor(rt["f1"][:, :], rt["absl"][:, :], m1v, ALU.max),
                               [(sac, sac_abs)])
                            V_(lambda e: e.tensor_tensor(rt["d1"][:, :], m1v, rt["lgs"][:, :], ALU.subtract))
                            V_(lambda e: e.tensor_scalar(rt["t1"][:, :], rt["f1"][:, :], JEPS, None, ALU.mult))
                            V_(lambda e: e.tensor_tensor(rt["keep1"][:, :], rt["d1"][:, :], rt["t1"][:, :], ALU.is_le))
                            V_(lambda e: e.tensor_scalar(rt["nm1"][:, :], rt["m1"][:, :], -1.0, None, ALU.mult))
                            sev_r1 = P.cnt(sev)
                            P.do("scalar", lambda e: e.activation(
                                rt["ex1"][:, :], rt["lgs"][:, :], AF.Exp, bias=rt["nm1"][:, :]),
                                waits=[(sev, sev_r1)], inc=sac)
                            sac_r1 = P.cnt(sac)
                            V_(lambda e: e.tensor_mul(rt["e1"][:, :], rt["keep1"][:, :], rt["ex1"][:, :]),
                               [(sac, sac_r1)])
                            V_(lambda e: e.reduce_sum(rt["sum1"][:, :], rt["e1"][:, :], axis=AX.X))
                            V_(lambda e: e.reciprocal(rt["mult1"][:, :], rt["sum1"][:, :]))
                            V_(lambda e: e.scalar_tensor_tensor(
                                rt["l2"][:, :], rt["is1"][:, :], -1e30, rt["lgs"][:, :], ALU.mult, ALU.add))
                            V_(lambda e: e.reduce_max(rt["m2"][:, :], rt["l2"][:, :], axis=AX.X))
                            V_(lambda e: e.tensor_tensor(rt["is2"][:, :], rt["l2"][:, :], m2v, ALU.is_equal))
                            V_(lambda e: e.tensor_tensor(rt["f2"][:, :], rt["absl"][:, :], m2v, ALU.max))
                            V_(lambda e: e.tensor_tensor(rt["d2"][:, :], m2v, rt["lgs"][:, :], ALU.subtract))
                            V_(lambda e: e.tensor_scalar(rt["t2"][:, :], rt["f2"][:, :], JEPS, None, ALU.mult))
                            V_(lambda e: e.tensor_tensor(rt["keep2"][:, :], rt["d2"][:, :], rt["t2"][:, :], ALU.is_le))
                            V_(lambda e: e.tensor_sub(rt["keep2"][:, :], rt["keep2"][:, :], rt["is1"][:, :]))
                            V_(lambda e: e.tensor_scalar(rt["nm2"][:, :], rt["m2"][:, :], -1.0, None, ALU.mult))
                            sev_r2 = P.cnt(sev)
                            P.do("scalar", lambda e: e.activation(
                                rt["ex2e"][:, :], rt["l2"][:, :], AF.Exp, bias=rt["nm2"][:, :]),
                                waits=[(sev, sev_r2)], inc=sac)
                            sac_r2 = P.cnt(sac)
                            V_(lambda e: e.tensor_mul(rt["e2"][:, :], rt["keep2"][:, :], rt["ex2e"][:, :]),
                               [(sac, sac_r2)])
                            V_(lambda e: e.reduce_sum(rt["sum2"][:, :], rt["e2"][:, :], axis=AX.X))
                            V_(lambda e: e.reciprocal(rt["mult2"][:, :], rt["sum2"][:, :]))
                            V_(lambda e: e.tensor_mul(rt["wte_tm"][:, :], rt["is1"][:, :], m1s))
                            V_(lambda e: e.scalar_tensor_tensor(
                                rt["e2"][:, :], rt["is2"][:, :], 1.0, m2s, ALU.mult, ALU.mult))
                            V_(lambda e: e.tensor_add(rt["wte_tm"][:, :], rt["wte_tm"][:, :], rt["e2"][:, :]))
                            sev_rt = P.cnt(sev)
                            sev_rt_prev = sev_rt
                            P.do("tensor", lambda e: e.transpose(
                                bass.AP(ps_rt, 128, [[512, NE], [1, 128]]),
                                rt["wte_tm"][:, :], ident_sb[:, :]),
                                waits=[(sev, sev_rt), (lds, lds_consts)], inc=smm)
                            smm_tr = P.cnt(smm)
                            P.do("scalar", lambda e, g=g, st=st: e.activation(
                                wte_fm[:, g * SG + st * 128: g * SG + (st + 1) * 128],
                                bass.AP(ps_rt, 128, [[512, NE], [1, 128]]), AF.Copy),
                                waits=[(smm, smm_tr)], inc=sac)
                            sac_wte[(g, st)] = P.cnt(sac)
                        sev_rt_last_of[g] = P.cnt(sev)
                        sac_rt_end[g] = P.cnt(sac)
                        sev_rt_end[g] = P.cnt(sev)


                    def recH(g):
                        first_norm = True
                        for j in range(6):
                            k = g * 6 + j
                            if j < 4:
                                for ol in range(4):
                                    oc = j * 4 + ol
                                    rot = rot_i[0] % 3
                                    rot_i[0] += 1
                                    pb = ps_rot[:, rot * 512:(rot + 1) * 512]
                                    w = [(ldg, ldg_slab[k]), (sev, sev_xn2_done[(g, 7)])]
                                    if rot in sev_evac_rot:
                                        w.append((sev, sev_evac_rot[rot]))
                                    for c in range(HC):
                                        P.do("tensor", lambda e, c=c, k=k, ol=ol, pb=pb: e.matmul(
                                            pb, wqbuf[:, k % 2, c, ol * 128:(ol + 1) * 128],
                                            xn2_grp[:, c, :], start=(c == 0), stop=(c == 7)),
                                            waits=(w if c == 0 else []), inc=smm)
                                    smm_oc = P.cnt(smm)
                                    smm_after_slab[k] = P.cnt(smm)
                                    bq = bass.AP(bqkv_sb, oc, [[24, 128], [0, SG]])
                                    w = [(smm, smm_oc), (lds, lds_small)]
                                    if oc < 8:
                                        dst = qst[:, oc % 2, :]
                                        if (g, oc - 2) in lds_qstore:
                                            w.append((lds, lds_qstore[(g, oc - 2)]))
                                        elif g >= 1 and (g - 1, 6 + oc) in lds_qstore:
                                            w.append((lds, lds_qstore[(g - 1, 6 + oc)]))
                                    else:
                                        dst = K_res[:, oc - 8, g * SG:(g + 1) * SG]
                                    P.do("vector", lambda e, pb=pb, dst=dst, bq=bq: e.tensor_tensor(
                                        dst, pb, bq, ALU.add), waits=w, inc=sev)
                                    sev_evac = P.cnt(sev)
                                    sev_evac_rot[rot] = sev_evac
                                    if oc < 8:
                                        P.do("sync", lambda e, g=g, oc=oc: e.dma_start(
                                            out=q_hbm[oc * 128:(oc + 1) * 128, g * SG:(g + 1) * SG],
                                            in_=qst[:, oc % 2, :]),
                                            waits=[(sev, sev_evac)], inc=lds, incn=16)
                                        lds_qstore[(g, oc)] = P.cnt(lds)
                                    w = [(sev, sev_evac)]
                                    if oc < 2:
                                        w.append((smm, smm_abcde_of.get(g + 1, smm_st2_of[g])))
                                    else:
                                        w.append((smm, smm_norm[(g, oc - 2)]))
                                    P.do("scalar", lambda e, dst=dst, oc=oc: e.activation(
                                        sqb[:, oc % 2, :], dst, AF.Square), waits=w, inc=sac)
                                    sac_sq = P.cnt(sac)
                                    w = [(sac, sac_sq)]
                                    if first_norm:
                                        w.append((sev, sev_rt_last_of[g]))
                                        w.append((sac, sac_wte[(g, 3)]))
                                        first_norm = False
                                    P.do("tensor", lambda e, oc=oc: e.matmul(
                                        ps_rt[0:2, 0:512], ones_bf[:, 1:3], sqb[:, oc % 2, :],
                                        start=True, stop=True), waits=w, inc=smm)
                                    smm_n = P.cnt(smm)
                                    P.do("vector", lambda e: e.reduce_max(
                                        rm_sb[:, :], ps_rt[0:2, 0:512], axis=AX.X),
                                        waits=[(smm, smm_n)], inc=sev)
                                    sev_rm = P.cnt(sev)
                                    P.do("tensor", lambda e: e.transpose(
                                        bass.AP(ps_rt, 384, [[512, 1], [1, 2]]),
                                        rm_sb[:, :], ident_sb[0:2, 0:2]),
                                        waits=[(sev, sev_rm)], inc=smm)
                                    smm_tr2 = P.cnt(smm)
                                    smm_norm[(g, oc)] = P.cnt(smm)
                                    P.do("vector", lambda e, oc=oc: e.tensor_tensor(
                                        maxflat[:, 2 * oc:2 * oc + 2], maxflat[:, 2 * oc:2 * oc + 2],
                                        bass.AP(ps_rt, 384, [[512, 1], [1, 2]]), ALU.max),
                                        waits=[(smm, smm_tr2)], inc=sev)
                            else:
                                for st in range(4):
                                    rot = rot_i[0] % 3
                                    rot_i[0] += 1
                                    pb = ps_rot[:, rot * 512:(rot + 1) * 512]
                                    w = [(ldg, ldg_slab[k]), (sev, sev_xn2_done[(g, 7)])]
                                    if rot in sev_evac_rot:
                                        w.append((sev, sev_evac_rot[rot]))
                                    for c in range(HC):
                                        P.do("tensor", lambda e, c=c, st=st, k=k, pb=pb: e.matmul(
                                            pb, xn2_grp[:, c, st * 128:(st + 1) * 128],
                                            wqbuf[:, k % 2, c, :], start=(c == 0), stop=(c == 7)),
                                            waits=(w if c == 0 else []), inc=smm)
                                    smm_v = P.cnt(smm)
                                    smm_after_slab[k] = P.cnt(smm)
                                    kt = g * 4 + st
                                    jj = j - 4
                                    dst = V_res[:, kt, jj * 520:(jj + 1) * 520].rearrange(
                                        "p (h x) -> p h x", x=65)[:, :, 0:64]
                                    P.do("vector", lambda e, pb=pb, dst=dst, jj=jj: e.tensor_tensor(
                                        dst, pb.rearrange("p (h x) -> p h x", x=64),
                                        vb_sb[:, jj * 512:(jj + 1) * 512].rearrange(
                                            "p (h x) -> p h x", x=64), ALU.add),
                                        waits=[(smm, smm_v), (sac, sac_vb)], inc=sev)
                                    sev_evac_rot[rot] = P.cnt(sev)
                            rec_slab(k + 2)
                        rec_precast(20)
                        sev_stab_end[g] = P.cnt(sev)
                        sac_sq_end[g] = P.cnt(sac)


                    sac_ab2_of = {}
                    sev_rt_prev = 0
                    recABCDE(0)
                    recF(0)
                    for g in range(NSG):
                        recG(g)
                        if g + 1 < NSG:
                            recABCDE(g + 1)
                        recH(g)
                        if g + 1 < NSG:
                            recF(g + 1)
                    rec_precast(1000)
                    sev_mf = P.cnt(sev)
                    P.do("vector", lambda e: e.tensor_mul(
                        maxflat[:, 32:48], maxflat[:, 0:16], maxflat[:, 16:32]),
                        waits=[(sev, sev_mf)], inc=sev)
                    sev_p = P.cnt(sev)
                    P.do("scalar", lambda e: e.activation(
                        maxflat[:, 48:64], maxflat[:, 32:48], AF.Sqrt),
                        waits=[(sev, sev_p)], inc=sac)
                    sac_s = P.cnt(sac)
                    P.do("vector", lambda e: e.tensor_scalar(
                        rowsB[:, 0:16], maxflat[:, 48:64], -0.130, None, ALU.mult),
                        waits=[(sac, sac_s)], inc=sev)
                    sev_c = P.cnt(sev)
                    P.do("tensor", lambda e: e.matmul(
                        ps_rt[:, 0:16], onesr_bf[:, :], rowsB[:, 0:16], start=True, stop=True),
                        waits=[(sev, sev_c)], inc=smm)
                    smm_cb = P.cnt(smm)
                    P.do("scalar", lambda e: e.activation(cbias[:, :], ps_rt[:, 0:16], AF.Copy),
                         waits=[(smm, smm_cb)], inc=sac)

                    if debug:
                        allw = P.snap(sems)
                        P.do("sync", lambda e: e.dma_start(
                            out=k_dbg.rearrange(FMRE, p=128), in_=K_res[:, :, :]),
                            waits=allw, inc=lds, incn=16)
                        P.do("sync", lambda e: e.dma_start(out=v_dbg, in_=V_res[:, :, :]),
                             waits=[], inc=lds, incn=16)
                        P.do("sync", lambda e: e.dma_start(out=wte_dbg, in_=wte_fm[:, :]),
                             waits=[], inc=lds, incn=16)
                        P.do("sync", lambda e: e.dma_start(out=cb_dbg, in_=cbias[:, :]),
                             waits=[], inc=lds, incn=16)

                    barrier1 = P.snap(sems)
                    P.emit(nc)

            if n_phases < 2:
                return nc

            with ExitStack() as es2:
                T2 = lambda *a: es2.enter_context(nc.sbuf_tensor(*a))
                woT = T2("woT", [128, HC, H], BF16)
                Qg = T2("Qg", [128, 2, HC, SG], BF16)
                xng = T2("xng", [128, 2, HC, SG], BF16)
                p_sb = T2("p_sb", [128, 2, 2, 1024], BF16)
                ctxg = T2("ctxg", [128, HC, SG], BF16)
                sst = T2("sst", [128, 2, SG], BF16)
                rbc = T2("rbc", [128, 2, SG], BF16)
                rec_sb = T2("rec_sb", [1, 2, SG], BF16)
                bout_sb = T2("bout_sb", [128, HC], F32)

                with ExitStack() as ps2:
                    PT2 = lambda *a: ps2.enter_context(nc.psum_tensor(*a))
                    ps_s = PT2("ps_s", [128, 2, 1024], F32)
                    ps_c = PT2("ps_c", [128, 2, 512], F32)
                    ps_o = PT2("ps_o", [128, 2, 512], F32)

                    P.do("gpsimd", lambda e: e.dma_start(
                        out=woT[:, :, :], in_=woutP.rearrange("c p o -> p c o")),
                        inc=ldg, incn=16)
                    ldg_wo = P.cnt(ldg)
                    P.do("sync", lambda e: e.dma_start(out=bout_sb[:, :], in_=boutP),
                         inc=lds, incn=16)
                    lds_bo = P.cnt(lds)

                    lds_q = {}
                    smm_qread = {}
                    sev_sst = {}
                    lds_sstore = {}

                    def rec_qload(q):
                        if q >= NSG:
                            return
                        w = []
                        if q >= 2:
                            w.append((smm, smm_qread[q - 2]))
                            w.append((sev, sev_sst.get((q - 2, HC - 1), 0)))
                        P.do("sync", lambda e, q=q: e.dma_start(
                            out=Qg[:, q % 2, :, :],
                            in_=q_hbm.rearrange(FMRE, p=128)[:, :, q * SG:(q + 1) * SG]),
                            waits=w, inc=lds, incn=16)
                        P.do("sync", lambda e, q=q: e.dma_start(
                            out=xng[:, q % 2, :, :],
                            in_=xn_hbm.rearrange(FMRE, p=128)[:, :, q * SG:(q + 1) * SG]),
                            waits=[], inc=lds, incn=16)
                        lds_q[q] = P.cnt(lds)

                    rec_qload(0)
                    rec_qload(1)

                    sac_exp = {}
                    smm_ctx_k = {}
                    sev_norm = {}
                    sac_norm = {}
                    sev_opev = {}

                    for q in range(NSG):
                        for hp in range(8):
                            def rec_scores(kt2, hp=hp, q=q):
                                w0 = [(lds, lds_q[q])]
                                for side in range(2):
                                    h = 2 * hp + side
                                    if kt2 >= 1:
                                        pk = sac_exp.get((q, h, kt2 - 1), 0)
                                    else:
                                        pk = sac_exp.get((q, h - 2, 15),
                                                         sac_exp.get((q - 1, NH - 2 + h, 15), 0))
                                    w = list(w0)
                                    if pk:
                                        w.append((sac, pk))
                                    if q >= 1 and hp == 0 and kt2 < 2:
                                        pj = sev_opev.get((q - 1, 6 + side), 0)
                                        if pj:
                                            w.append((sev, pj))
                                    base = 64 * side
                                    for half in range(2):
                                        kt = kt2 * 2 + half
                                        P.do("tensor", lambda e, hp=hp, kt=kt, base=base,
                                             side=side, half=half, q=q, kt2=kt2: e.matmul(
                                                 ps_s[:, side, half * 512:(half + 1) * 512],
                                                 K_res[base:base + 64, hp, kt * 128:(kt + 1) * 128],
                                                 Qg[base:base + 64, q % 2, hp, :],
                                                 start=True, stop=True, tile_position=(base, 0)),
                                            waits=(w if half == 0 else []), inc=smm)
                                    smm_sc = P.cnt(smm)
                                    h = 2 * hp + side
                                    w = [(smm, smm_sc)]
                                    pc = smm_ctx_k.get((q, h, kt2 - 2), 0)
                                    if pc:
                                        w.append((smm, pc))
                                    P.do("scalar", lambda e, side=side, kt2=kt2, h=h: e.activation(
                                        p_sb[:, side, kt2 % 2, :], ps_s[:, side, :], AF.Exp,
                                        bias=cbias[:, h:h + 1], scale=0.125),
                                        waits=w, inc=sac)
                                    sac_exp[(q, h, kt2)] = P.cnt(sac)

                            def rec_ctx(kt2, hp=hp, q=q):
                                for side in range(2):
                                    h = 2 * hp + side
                                    wc = [(sac, sac_exp[(q, h, kt2)])]
                                    if kt2 == 0:
                                        pn = sev_norm.get((q, h - 2), sev_norm.get((q - 1, NH - 2 + h), 0))
                                        if pn:
                                            wc.append((sev, pn))
                                    for half in range(2):
                                        kt = kt2 * 2 + half
                                        P.do("tensor", lambda e, h=h, kt=kt, kt2=kt2,
                                             side=side, half=half: e.matmul(
                                                 ps_c[0:65, side, :],
                                                 V_res[:, kt, h * 65:(h + 1) * 65],
                                                 p_sb[:, side, kt2 % 2, half * 512:(half + 1) * 512],
                                                 start=(kt == 0), stop=(kt == 31)),
                                            waits=(wc if half == 0 else []), inc=smm)
                                    smm_ctx_k[(q, h, kt2)] = P.cnt(smm)

                            rec_scores(0)
                            for kt2 in range(1, 16):
                                rec_scores(kt2)
                                rec_ctx(kt2 - 1)
                            rec_ctx(15)
                            smm_ctx = P.cnt(smm)
                            for side in range(2):
                                h = 2 * hp + side
                                base = 64 * side

                                def _recip(e, side=side):
                                    with nc.allow_low_precision(
                                            reason="softmax recip feeds bf16 matmul"):
                                        return e.reciprocal(rec_sb[:, side, :],
                                                            ps_c[64:65, side, :])
                                P.do("vector", _recip, waits=[(smm, smm_ctx)], inc=sev)
                                sev_rc = P.cnt(sev)
                                wb = [(sev, sev_rc), (ldg, ldg_consts)]
                                pn = sac_norm.get((q, h - 2), sac_norm.get((q - 1, NH - 2 + h), 0))
                                if pn:
                                    wb.append((sac, pn))
                                pnm = sev_norm.get((q, h - 2), sev_norm.get((q - 1, NH - 2 + h), 0))
                                if pnm:
                                    wb.append((sev, pnm))
                                po = sev_opev.get((q - 1, 6 + side), 0)
                                if hp == 0 and po:
                                    wb.append((sev, po))
                                P.do("tensor", lambda e, side=side: e.matmul(
                                    ps_o[:, side, :], onesr_bf[:, :], rec_sb[:, side, :],
                                    start=True, stop=True), waits=wb, inc=smm)
                                smm_bc = P.cnt(smm)
                                P.do("scalar", lambda e, side=side: e.activation(
                                    rbc[:, side, :], ps_o[:, side, :], AF.Copy),
                                    waits=[(smm, smm_bc)], inc=sac)
                                sac_bc = P.cnt(sac)
                                sac_norm[(q, h)] = sac_bc
                                P.do("vector", lambda e, side=side, hp=hp, base=base: e.tensor_mul(
                                    ctxg[base:base + 64, hp, :],
                                    ps_c[0:64, side, :], rbc[base:base + 64, side, :]),
                                    waits=[(sac, sac_bc)], inc=sev)
                                sev_norm[(q, h)] = P.cnt(sev)
                        smm_qread[q] = P.cnt(smm)
                        sev_ctx_all = P.cnt(sev)
                        for oc in range(HC):
                            pb = ps_o[:, oc % 2, :]
                            w = [(sev, sev_ctx_all), (ldg, ldg_wo), (lds, lds_q[q])]
                            pp = sev_sst.get((q, oc - 2), 0)
                            if pp:
                                w.append((sev, pp))
                            if oc < 2:
                                w.append((sac, sac_norm[(q, NH - 2 + oc)]))
                                w.append((sev, sev_norm[(q, NH - 2 + oc)]))
                            for c in range(HC):
                                P.do("tensor", lambda e, c=c, oc=oc, pb=pb: e.matmul(
                                    pb, woT[:, c, oc * 128:(oc + 1) * 128], ctxg[:, c, :],
                                    start=(c == 0), stop=(c == 7)),
                                    waits=(w if c == 0 else []), inc=smm)
                            smm_o = P.cnt(smm)
                            w = [(smm, smm_o), (lds, lds_bo)]
                            pst = lds_sstore.get((q, oc - 2), lds_sstore.get((q - 1, 6 + oc), 0))
                            if pst:
                                w.append((lds, pst))
                            P.do("vector", lambda e, oc=oc, pb=pb, q=q: e.scalar_tensor_tensor(
                                sst[:, oc % 2, :], pb, bout_sb[:, oc:oc + 1],
                                xng[:, q % 2, oc, :], ALU.add, ALU.add), waits=w, inc=sev)
                            sev_sst[(q, oc)] = P.cnt(sev)
                            sev_opev[(q, 6 + oc % 2)] = P.cnt(sev)
                            P.do("sync", lambda e, oc=oc, q=q: e.dma_start(
                                out=s_hbm[oc * 128:(oc + 1) * 128, q * SG:(q + 1) * SG],
                                in_=sst[:, oc % 2, :]),
                                waits=[(sev, sev_sst[(q, oc)])], inc=lds, incn=16)
                            lds_sstore[(q, oc)] = P.cnt(lds)
                        rec_qload(q + 2)

                    barrier2 = P.snap(sems)
                    P.emit(nc, barrier_waits=barrier1)

        if n_phases < 3:
            return nc

        NCAP = 384
        NJC = NCAP // 128
        with ExitStack() as es3:
            T3 = lambda *a: es3.enter_context(nc.sbuf_tensor(*a))
            xn_tm = T3("xn_tm", [128, 32, 1024], BF16)
            h_c = T3("h_c", [128, 32, NCAP], BF16)
            acc = T3("acc", [128, HC, MG], F32)
            w13buf = T3("w13buf", [128, 2, 2, HC, FS], BF16)
            w2buf = T3("w2buf", [128, 2, 2, 1024], BF16)
            xgat = T3("xgat", [128, HC, NCAP], BF16)
            outT = T3("outT", [128, NJC, 1024], BF16)
            SelG = T3("SelG", [128, 8, NCAP], BF16)
            Scat = T3("Scat", [128, NJC, MG], BF16)
            poss = T3("poss", [8, 2, MG], F32)
            maskr = T3("maskr", [8, MG], F32)
            poss_tm = T3("poss_tm", [128, 8, 8], F32)
            poss_bc = T3("poss_bc", [128, MG], F32)
            wte_bc = T3("wte_bc", [128, MG], BF16)
            iseqT = T3("iseqT", [128, MG], BF16)
            jio_sb = T3("jio_sb", [128, NCAP], F32)
            pio_sb = T3("pio_sb", [128, 1], F32)
            pjc_sb = T3("pjc_sb", [128, 1], F32)
            sel_f = T3("sel_f", [NE, NE * 128], F32)
            sel_bf2 = T3("sel_bf2", [NE, NE * 128], BF16)
            id3 = T3("id3", [128, 128], F32)
            silu_sb3 = T3("silu_sb3", [128, 2, NCAP], BF16)

            with ExitStack() as ps3:
                ps_all = ps3.enter_context(nc.psum_tensor("ps_all", [128, 4096], F32))
                B512 = lambda b: ps_all[:, b * 512:(b + 1) * 512]
                BR = lambda b, n: ps_all[:, b * 512:(b + n) * 512]

                P.do("gpsimd", lambda e: e.dma_start(out=sel_bf2[:, :], in_=selP),
                     inc=ldg, incn=16)
                ldg_sel3 = P.cnt(ldg)
                P.do("sync", lambda e: e.dma_start(out=sel_f[:, :], in_=selP), inc=lds, incn=16)
                P.do("sync", lambda e: e.dma_start(out=jio_sb[:, :], in_=jiotaP), inc=lds, incn=16)
                P.do("sync", lambda e: e.dma_start(out=pio_sb[:, :], in_=piotaP), inc=lds, incn=16)
                P.do("sync", lambda e: e.dma_start(out=id3[:, :], in_=identP), inc=lds, incn=16)
                for t in range(32):
                    P.do("sync", lambda e, t=t: e.dma_start(
                        out=xn_tm[:, t, :], in_=xn_hbm[:, t * 128:(t + 1) * 128],
                        transpose=True), inc=lds, incn=16)
                lds_xtm = P.cnt(lds)
                lds_c3 = P.cnt(lds)

                NW13 = NMG * NE * NFS
                ldg_13 = {}
                lds_2 = {}
                smm_13_done = {}
                smm_2_done = {}

                def rec_w13(kk):
                    if kk >= NW13:
                        return
                    gi, rem = divmod(kk, NE * NFS)
                    ei, fs = divmod(rem, NFS)
                    w = []
                    if kk >= 2:
                        w.append((smm, smm_13_done.get(kk - 2, 0)))
                    P.do("gpsimd", lambda e, ei=ei, fs=fs, kk=kk: e.dma_start(
                        out=w13buf[:, :, kk % 2], in_=w13B[ei, fs]),
                        waits=w, inc=ldg, incn=16)
                    ldg_13[kk] = P.cnt(ldg)

                def rec_w2(kk):
                    if kk >= NMG * NE * 16:
                        return
                    gi, rem = divmod(kk, NE * 16)
                    ei, f2 = divmod(rem, 16)
                    w = []
                    if kk >= 2:
                        w.append((smm, smm_2_done.get(kk - 2, 0)))
                    P.do("sync", lambda e, ei=ei, f2=f2, kk=kk: e.dma_start(
                        out=w2buf[:, kk % 2], in_=w2B[ei, f2 * 2:f2 * 2 + 2].rearrange(
                            "f p o -> p f o")), waits=w, inc=lds, incn=16)
                    lds_2[kk] = P.cnt(lds)

                rec_w13(0)
                rec_w13(1)

                sev_h = {}
                sac_silu = {}
                prev_gather_smm = [0]
                prev_scatter_smm = [0]

                for gi in range(NMG):
                    sac_gi_start = P.cnt(sac)
                    gsl = slice(gi * MG, (gi + 1) * MG)
                    P.do("vector", lambda e, gsl=gsl: e.tensor_scalar(
                        maskr[:, :], wte_fm[0:NE, gsl], 0.0, None, ALU.is_gt), inc=sev)
                    cur = 0
                    P.do("vector", lambda e: e.tensor_copy(poss[:, 0, :], maskr[:, :]), inc=sev)
                    sh = 1
                    while sh < MG:
                        P.do("vector", lambda e, cur=cur, sh=sh: e.tensor_copy(
                            poss[:, 1 - cur, 0:sh], poss[:, cur, 0:sh]), inc=sev)
                        P.do("vector", lambda e, cur=cur, sh=sh: e.tensor_add(
                            poss[:, 1 - cur, sh:MG], poss[:, cur, sh:MG], poss[:, cur, 0:MG - sh]),
                            inc=sev)
                        cur = 1 - cur
                        sh *= 2
                    oth = 1 - cur
                    P.do("vector", lambda e, cur=cur: e.tensor_scalar(
                        poss[:, cur, :], poss[:, cur, :], -1.0, None, ALU.add), inc=sev)
                    P.do("vector", lambda e, cur=cur, oth=oth: e.tensor_mul(
                        poss[:, oth, :], poss[:, cur, :], maskr[:, :]), inc=sev)
                    P.do("vector", lambda e: e.tensor_scalar(
                        maskr[:, :], maskr[:, :], -4096.0, 4096.0, ALU.mult, ALU.add), inc=sev)
                    P.do("vector", lambda e, oth=oth: e.tensor_add(
                        poss[:, oth, :], poss[:, oth, :], maskr[:, :]), inc=sev)
                    sev_poss = P.cnt(sev)
                    for tc in range(8):
                        w = [(sev, sev_poss), (lds, lds_c3)]
                        if gi > 0 or tc > 0:
                            w.append((sev, P.cnt(sev)))
                        P.do("tensor", lambda e, tc=tc, oth=oth: e.transpose(
                            bass.AP(ps_all, 6 * 512, [[4096, 128], [1, 8]]),
                            poss[:, oth, tc * 128:(tc + 1) * 128], id3[0:8, 0:8]),
                            waits=w, inc=smm)
                        smm_tr = P.cnt(smm)
                        P.do("vector", lambda e, tc=tc: e.tensor_copy(
                            poss_tm[:, tc, :], bass.AP(ps_all, 6 * 512, [[4096, 128], [1, 8]])),
                            waits=[(smm, smm_tr)], inc=sev)
                    sev_ptm = P.cnt(sev)

                    state = {}

                    def rec_head(ei):
                        sev_e_start = P.cnt(sev)
                        sac_e_start = P.cnt(sac)
                        oth = state["oth"]
                        gsl = state["gsl"]
                        gi = state["gi"]
                        bb = 6 if ei == 0 else 0
                        for half in range(2):
                            P.do("tensor", lambda e, ei=ei, half=half, oth=oth, bb=bb: e.matmul(
                                B512(bb + half), sel_f[:, ei * 128:(ei + 1) * 128],
                                poss[:, oth, half * 512:(half + 1) * 512],
                                start=True, stop=True),
                                waits=[(sev, sev_e_start), (sac, sac_e_start),
                                       (lds, lds_c3)] if half == 0 else [],
                                inc=smm)
                        smm_pb = P.cnt(smm)
                        P.do("scalar", lambda e, bb=bb: e.activation(
                            poss_bc[:, :], BR(bb, 2), AF.Copy),
                            waits=[(smm, smm_pb)], inc=sac)
                        sac_pb = P.cnt(sac)
                        for half in range(2):
                            P.do("tensor", lambda e, ei=ei, half=half, gsl=gsl, bb=bb: e.matmul(
                                B512(bb + half), sel_bf2[:, ei * 128:(ei + 1) * 128],
                                wte_fm[0:NE, gsl][:, half * 512:(half + 1) * 512],
                                start=True, stop=True),
                                waits=[(sac, sac_pb), (ldg, ldg_sel3)] if half == 0 else [],
                                inc=smm)
                        smm_wb = P.cnt(smm)
                        P.do("scalar", lambda e, bb=bb: e.activation(
                            wte_bc[:, :], BR(bb, 2), AF.Copy),
                            waits=[(smm, smm_wb)], inc=sac)
                        sac_wb = P.cnt(sac)
                        for tc in range(8):
                            pv = bass.AP(poss_tm, tc * 8 + ei, [[64, 128], [0, NCAP]])
                            P.do("vector", lambda e, tc=tc, pv=pv: e.tensor_tensor(
                                SelG[:, tc, :], jio_sb[:, :], pv, ALU.is_equal),
                                waits=[(sev, state["sev_ptm"]), (smm, prev_gather_smm[0])],
                                inc=sev)
                        sev_selg = P.cnt(sev)
                        sac_xg = {}
                        for hc in range(HC):
                            rot = hc % 2
                            w = [(lds, lds_xtm), (sev, sev_selg)]
                            pg = sac_xg.get(hc - 2, 0)
                            if pg:
                                w.append((sac, pg))
                            if hc < 2:
                                w.append((sac, sac_wb))
                            for tc in range(8):
                                P.do("tensor", lambda e, hc=hc, tc=tc, rot=rot, gi=gi: e.matmul(
                                    ps_all[:, rot * 512:rot * 512 + NCAP],
                                    xn_tm[:, gi * 8 + tc, hc * 128:(hc + 1) * 128],
                                    SelG[:, tc, :], start=(tc == 0), stop=(tc == 7)),
                                    waits=(w if tc == 0 else []), inc=smm)
                            smm_g = P.cnt(smm)
                            P.do("scalar", lambda e, hc=hc, rot=rot: e.activation(
                                xgat[:, hc, :], ps_all[:, rot * 512:rot * 512 + NCAP], AF.Copy),
                                waits=[(smm, smm_g)], inc=sac)
                            sac_xg[hc] = P.cnt(sac)
                        state["sac_xg_all", ei] = P.cnt(sac)
                        state["sac_wb", ei] = sac_wb
                        state["sac_pb", ei] = sac_pb
                        prev_gather_smm[0] = P.cnt(smm)

                    def rec_mid(ei):
                        gi = state["gi"]
                        sac_xg_all = state["sac_xg_all", ei]
                        sev_e_start = P.cnt(sev)
                        kk0 = (gi * NE + ei) * NFS
                        for fs in range(NFS):
                            kk = kk0 + fs
                            for fcl in range(2):
                                fc = fs * 2 + fcl
                                jj = kk * 2 + fcl
                                pa = B512(2 + jj % 2)
                                pb = B512(4 + jj % 2)
                                wA = [(ldg, ldg_13[kk]), (sac, sac_xg_all)]
                                pv = sev_h.get(jj - 2, 0)
                                if pv:
                                    wA.append((sev, pv))
                                if fs == 0:
                                    wA.append((sev, sev_e_start))
                                for c in range(HC):
                                    P.do("tensor", lambda e, c=c, kk=kk, fcl=fcl, pa=pa: e.matmul(
                                        pa[:, 0:NCAP],
                                        w13buf[:, 0, kk % 2, c, fcl * 128:(fcl + 1) * 128],
                                        xgat[:, c, :], start=(c == 0), stop=(c == 7)),
                                        waits=(wA if c == 0 else []), inc=smm)
                                smm_a = P.cnt(smm)
                                wB = []
                                if pv:
                                    wB.append((sev, pv))
                                if fs == 0:
                                    wB.append((sev, sev_e_start))
                                for c in range(HC):
                                    P.do("tensor", lambda e, c=c, kk=kk, fcl=fcl, pb=pb: e.matmul(
                                        pb[:, 0:NCAP],
                                        w13buf[:, 1, kk % 2, c, fcl * 128:(fcl + 1) * 128],
                                        xgat[:, c, :], start=(c == 0), stop=(c == 7)),
                                        waits=(wB if c == 0 else []), inc=smm)
                                smm_b = P.cnt(smm)
                                smm_13_done[kk] = P.cnt(smm)
                                w = [(smm, smm_a)]
                                pv5 = sev_h.get(jj - 2, 0)
                                if pv5:
                                    w.append((sev, pv5))
                                P.do("scalar", lambda e, pa=pa, jj=jj: e.activation(
                                    silu_sb3[:, jj % 2, :], pa[:, 0:NCAP], AF.Silu),
                                    waits=w, inc=sac)
                                sac_silu[jj] = P.cnt(sac)
                                P.do("vector", lambda e, pb=pb, jj=jj, fc=fc: e.tensor_mul(
                                    h_c[:, fc, :], silu_sb3[:, jj % 2, :], pb[:, 0:NCAP]),
                                    waits=[(sac, sac_silu[jj]), (smm, smm_b)], inc=sev)
                                sev_h[jj] = P.cnt(sev)
                            rec_w13(kk + 2)
                        sev_h_all = P.cnt(sev)

                        kk20 = (gi * NE + ei) * 16
                        rec_w2(kk20)
                        rec_w2(kk20 + 1)
                        for f2 in range(16):
                            kk = kk20 + f2
                            w0 = [(lds, lds_2[kk]), (sev, sev_h_all), (sac, sac_xg_all)]
                            for fl in range(2):
                                fc = f2 * 2 + fl
                                for jc in range(NJC):
                                    for oh in range(2):
                                        P.do("tensor", lambda e, fc=fc, fl=fl, jc=jc, oh=oh, kk=kk: e.matmul(
                                            ps_all[:, (jc * 2 + oh) * 512:(jc * 2 + oh + 1) * 512],
                                            h_c[:, fc, jc * 128:(jc + 1) * 128],
                                            w2buf[:, kk % 2, fl, oh * 512:(oh + 1) * 512],
                                            start=(fc == 0), stop=(fc == 31)),
                                            waits=(w0 if (fl == 0 and jc == 0 and oh == 0) else []), inc=smm)
                            smm_2_done[kk] = P.cnt(smm)
                            rec_w2(kk + 2)
                        smm_w2_all = P.cnt(smm)
                        for jc in range(NJC):
                            P.do("scalar", lambda e, jc=jc: e.activation(
                                outT[:, jc, :], BR(jc * 2, 2), AF.Copy),
                                waits=[(smm, smm_w2_all)], inc=sac)
                        state["sac_oT_all", ei] = P.cnt(sac)
                        for jc in range(NJC):
                            P.do("vector", lambda e, jc=jc: e.tensor_scalar(
                                pjc_sb[:, :], pio_sb[:, :], float(jc * 128), None, ALU.add),
                                waits=[(lds, lds_c3)], inc=sev)
                            pv = bass.AP(pjc_sb, 0, [[1, 128], [0, MG]])
                            P.do("vector", lambda e, pv=pv: e.tensor_tensor(
                                iseqT[:, :], poss_bc[:, :], pv, ALU.is_equal),
                                waits=[(sac, state["sac_pb", ei])], inc=sev)
                            P.do("vector", lambda e, jc=jc: e.tensor_mul(
                                Scat[:, jc, :], iseqT[:, :], wte_bc[:, :]),
                                waits=[(sac, state["sac_wb", ei]), (smm, prev_scatter_smm[0])], inc=sev)
                        state["sev_scat_all", ei] = P.cnt(sev)

                    def rec_scatter(ei):
                        gi = state["gi"]
                        sev_scat_all = state["sev_scat_all", ei]
                        sac_oT_all = state["sac_oT_all", ei]
                        sev_aevac = {}
                        for oc in range(HC):
                            for th in range(2):
                                rot = (oc * 2 + th) % 2
                                w = [(sev, sev_scat_all), (sac, sac_oT_all)]
                                pe_ = sev_aevac.get((oc * 2 + th) - 2, 0)
                                if pe_:
                                    w.append((sev, pe_))
                                for jc in range(NJC):
                                    P.do("tensor", lambda e, oc=oc, th=th, jc=jc, rot=rot: e.matmul(
                                        B512(6 + rot),
                                        outT[:, jc, oc * 128:(oc + 1) * 128],
                                        Scat[:, jc, th * 512:(th + 1) * 512],
                                        start=(jc == 0), stop=(jc == NJC - 1)),
                                        waits=(w if jc == 0 else []), inc=smm)
                                smm_sc2 = P.cnt(smm)
                                wv = [(smm, smm_sc2)]
                                if ei == 0 and state["gi"] >= 1:
                                    wv.append((sac, state["sac_gi_start"]))
                                if ei == 0:
                                    P.do("vector", lambda e, oc=oc, th=th, rot=rot: e.tensor_copy(
                                        acc[:, oc, th * 512:(th + 1) * 512], B512(6 + rot)),
                                        waits=wv, inc=sev)
                                else:
                                    P.do("vector", lambda e, oc=oc, th=th, rot=rot: e.tensor_add(
                                        acc[:, oc, th * 512:(th + 1) * 512],
                                        acc[:, oc, th * 512:(th + 1) * 512], B512(6 + rot)),
                                        waits=wv, inc=sev)
                                sev_aevac[oc * 2 + th] = P.cnt(sev)
                        prev_scatter_smm[0] = P.cnt(smm)

                    state["gi"] = gi
                    state["oth"] = oth
                    state["gsl"] = gsl
                    state["sev_ptm"] = sev_ptm
                    state["sac_gi_start"] = sac_gi_start
                    rec_head(0)
                    for ei in range(NE):
                        rec_mid(ei)
                        if ei + 1 < NE:
                            rec_head(ei + 1)
                        rec_scatter(ei)
                    sev_acc_all = P.cnt(sev)
                    g_view = bass.AP(xgat, 0, [[HC * NCAP, 128], [1, MG]])
                    for oc in range(HC):
                        P.do("scalar", lambda e, oc=oc: e.activation(
                            g_view, acc[:, oc, :], AF.Silu),
                            waits=[(sev, sev_acc_all), (lds, P.cnt(lds))], inc=sac)
                        sac_g = P.cnt(sac)
                        P.do("sync", lambda e, oc=oc, gi=gi: e.dma_start(
                            out=g_hbm[oc * 128:(oc + 1) * 128, gi * MG:(gi + 1) * MG],
                            in_=g_view), waits=[(sac, sac_g)], inc=lds, incn=16)

                barrier3 = P.snap(sems)
                P.emit(nc, barrier_waits=barrier2)

        if n_phases < 4:
            return nc

        with ExitStack() as es4:
            T4 = lambda *a: es4.enter_context(nc.sbuf_tensor(*a))
            xt = T4("xt", [128, 2, S], F32)
            st4 = T4("st4", [128, 2, S], BF16)
            gt4 = T4("gt4", [128, 2, S], BF16)
            tt4 = T4("tt4", [128, S], F32)
            yt4 = T4("yt4", [128, 2, S], F32)

            sev_y = {}
            lds_tile = {}
            lds_yst = {}
            for i in range(HC):
                w = []
                if i >= 2:
                    w.append((sev, sev_y[i - 2]))
                P.do("sync", lambda e, i=i: e.dma_start(
                    out=xt[:, i % 2, :], in_=x_fm[i * 128:(i + 1) * 128, :]),
                    waits=w, inc=lds, incn=16)
                P.do("sync", lambda e, i=i: e.dma_start(
                    out=st4[:, i % 2, :], in_=s_hbm[i * 128:(i + 1) * 128, :]),
                    waits=[], inc=lds, incn=16)
                P.do("sync", lambda e, i=i: e.dma_start(
                    out=gt4[:, i % 2, :], in_=g_hbm[i * 128:(i + 1) * 128, :]),
                    waits=[], inc=lds, incn=16)
                lds_tile[i] = P.cnt(lds)
                P.do("vector", lambda e, i=i: e.tensor_mul(
                    tt4[:, :], st4[:, i % 2, :], gt4[:, i % 2, :]),
                    waits=[(lds, lds_tile[i])], inc=sev)
                w = []
                if i >= 2 and lds_yst.get(i - 2, 0):
                    w.append((lds, lds_yst[i - 2]))
                P.do("vector", lambda e, i=i: e.tensor_add(
                    yt4[:, i % 2, :], xt[:, i % 2, :], tt4[:, :]), waits=w, inc=sev)
                sev_y[i] = P.cnt(sev)
                P.do("sync", lambda e, i=i: e.dma_start(
                    out=y_hbm[i * 128:(i + 1) * 128, :],
                    in_=yt4[:, i % 2, :]), waits=[(sev, sev_y[i])], inc=lds, incn=16)
                lds_yst[i] = P.cnt(lds)

            P.do("sync", lambda e: e.nop(), waits=[(lds, P.cnt(lds))])
            P.emit(nc, barrier_waits=barrier3)

    return nc


def _pack(inputs):
    x = np.ascontiguousarray(inputs["L0_attn_hidden_states"])
    f32 = np.float32
    d = {}
    d["wqkvP"] = np.ascontiguousarray(
        inputs["L0_attn_W_qkv"].T.reshape(HC, 128, 6, 512).transpose(2, 1, 0, 3), dtype=f32)
    d["woutP"] = np.ascontiguousarray(
        inputs["L0_attn_W_out"].T.reshape(HC, 128, H), dtype=f32)
    w1p = inputs["L0_ffn_w1"].transpose(0, 2, 1).reshape(NE, HC, 128, NFS, FS).transpose(0, 3, 2, 1, 4)
    w3p = inputs["L0_ffn_w3"].transpose(0, 2, 1).reshape(NE, HC, 128, NFS, FS).transpose(0, 3, 2, 1, 4)
    d["w13P"] = np.ascontiguousarray(
        np.stack([w1p, w3p], axis=3), dtype=f32)
    d["w2P"] = np.ascontiguousarray(
        inputs["L0_ffn_w2"].transpose(0, 2, 1).reshape(NE, 32, 128, H), dtype=f32)
    d["gateP"] = np.ascontiguousarray(
        inputs["L0_ffn_gate_w"].T.reshape(HC, 128, NE), dtype=f32)
    d["bqkvP"] = np.ascontiguousarray(inputs["L0_attn_b_qkv"].reshape(24, 128).T, dtype=f32)
    d["bvrow"] = np.ascontiguousarray(inputs["L0_attn_b_qkv"][2048:].reshape(1, H), dtype=f32)
    d["boutP"] = np.ascontiguousarray(inputs["L0_attn_b_out"].reshape(HC, 128).T, dtype=f32)
    d["gammaP"] = np.ascontiguousarray(inputs["L0_attn_ln_gamma"].reshape(HC, 128).T, dtype=f32)
    d["betaP"] = np.ascontiguousarray(inputs["L0_attn_ln_beta"].reshape(HC, 128).T, dtype=f32)
    d["identP"] = np.eye(128, dtype=f32)
    o4 = np.zeros((128, 4), f32)
    o4[:, 0] = 1.0
    o4[0:64, 1] = 1.0
    o4[64:128, 2] = 1.0
    d["onesP"] = o4
    d["onesrP"] = np.ones((1, 128), f32)
    sel = np.zeros((NE, NE * 128), f32)
    for e in range(NE):
        sel[e, e * 128:(e + 1) * 128] = 1.0
    d["selP"] = sel
    d["jiotaP"] = np.broadcast_to(np.arange(384, dtype=f32)[None, :], (128, 384)).copy()
    d["piotaP"] = np.arange(128, dtype=f32).reshape(128, 1)
    xs = [np.ascontiguousarray(x[c].T, dtype=f32) for c in range(8)]
    return d, xs


def kernel(**inputs):
    shared, xs = _pack(inputs)
    nc = build(debug=False, n_phases=4)
    in_maps = [dict(shared, x_fm=xs[c]) for c in range(8)]
    res = run_bass_kernel_spmd(nc, in_maps, core_ids=list(range(8)))
    out = np.stack([res.results[c]["y"].T for c in range(8)])
    return np.ascontiguousarray(out, dtype=np.float32)


# version 17
# speedup vs baseline: 1.0289x; 1.0289x over previous
import numpy as np
import concourse.bass as bass
import concourse.mybir as mybir
from concourse.bass_utils import run_bass_kernel_spmd
from contextlib import ExitStack

F32 = mybir.dt.float32
BF16 = mybir.dt.bfloat16
AF = mybir.ActivationFunctionType
ALU = mybir.AluOpType
AX = mybir.AxisListType

S = 4096
H = 1024
NH = 16
HD = 64
NE = 8
FF = 4096
HC = 8
SG = 512
NSG = S // SG
MG = 1024
NMG = S // MG
FS = 256
NFS = FF // FS
EPS = 1e-5
JEPS = 0.02

ENGS = ("tensor", "vector", "scalar", "gpsimd", "sync")


class Prog:

    def __init__(self):
        self.items = {e: [] for e in ENGS}
        self.ctr = {}

    def cnt(self, sem):
        return self.ctr.get(id(sem), 0)

    def snap(self, sems):
        return [(s, self.cnt(s)) for s in sems]

    def do(self, eng, fn, waits=(), inc=None, incn=1):
        ws = [(s, v) for (s, v) in waits if v > 0]
        if inc is not None and eng in ("vector", "scalar"):
            pv = self.cnt(inc)
            if pv > 0:
                ws.append((inc, pv))
        if inc is not None:
            self.ctr[id(inc)] = self.ctr.get(id(inc), 0) + incn
        self.items[eng].append((fn, ws, inc, incn))
        return self.ctr.get(id(inc), 0) if inc is not None else None

    def emit(self, nc, barrier_waits=None):
        bw = [(s, v) for (s, v) in (barrier_waits or []) if v > 0]
        with nc.Block() as block:
            for eng in ENGS:
                items = self.items[eng]
                if not items and not bw:
                    continue

                def make(items=items):
                    def run(e):
                        seen = {}
                        for (s, v) in bw:
                            e.wait_ge(s, v)
                            seen[id(s)] = v
                        for (fn, ws, inc, incn) in items:
                            for (s, v) in ws:
                                if seen.get(id(s), -1) < v:
                                    e.wait_ge(s, v)
                                    seen[id(s)] = v
                            ins = fn(e)
                            if inc is not None:
                                ins.then_inc(inc, incn)
                    return run

                getattr(block, eng)(make())
        self.items = {e: [] for e in ENGS}


def build(debug=False, n_phases=4):
    nc = bass.Bass("TRN2", target_bir_lowering=False)

    x_fm = nc.dram_tensor("x_fm", [H, S], F32, kind="ExternalInput").ap()
    wqkvP = nc.dram_tensor("wqkvP", [6, 128, HC, 512], F32, kind="ExternalInput").ap()
    woutP = nc.dram_tensor("woutP", [HC, 128, H], F32, kind="ExternalInput").ap()
    w13P = nc.dram_tensor("w13P", [NE, NFS, 128, 2, HC, FS], F32, kind="ExternalInput").ap()
    w2P = nc.dram_tensor("w2P", [NE, 32, 128, 1024], F32, kind="ExternalInput").ap()
    gateP = nc.dram_tensor("gateP", [HC, 128, NE], F32, kind="ExternalInput").ap()
    bqkvP = nc.dram_tensor("bqkvP", [128, 24], F32, kind="ExternalInput").ap()
    bvrow = nc.dram_tensor("bvrow", [1, H], F32, kind="ExternalInput").ap()
    boutP = nc.dram_tensor("boutP", [128, HC], F32, kind="ExternalInput").ap()
    gammaP = nc.dram_tensor("gammaP", [128, HC], F32, kind="ExternalInput").ap()
    betaP = nc.dram_tensor("betaP", [128, HC], F32, kind="ExternalInput").ap()
    identP = nc.dram_tensor("identP", [128, 128], F32, kind="ExternalInput").ap()
    onesP = nc.dram_tensor("onesP", [128, 4], F32, kind="ExternalInput").ap()
    onesrP = nc.dram_tensor("onesrP", [1, 128], F32, kind="ExternalInput").ap()
    selP = nc.dram_tensor("selP", [NE, NE * 128], F32, kind="ExternalInput").ap()
    jiotaP = nc.dram_tensor("jiotaP", [128, 384], F32, kind="ExternalInput").ap()
    piotaP = nc.dram_tensor("piotaP", [128, 1], F32, kind="ExternalInput").ap()

    w13B = nc.dram_tensor("w13B", [NE, NFS, 128, 2, HC, FS], BF16).ap()
    w2B = nc.dram_tensor("w2B", [NE, 32, 128, 1024], BF16).ap()
    sk = {"kind": "ExternalOutput"} if debug else {}
    xn_hbm = nc.dram_tensor("xn_hbm", [H, S], BF16, **sk).ap()
    q_hbm = nc.dram_tensor("q_hbm", [H, S], BF16, **sk).ap()
    s_hbm = nc.dram_tensor("s_hbm", [H, S], BF16, **sk).ap()
    g_hbm = nc.dram_tensor("g_hbm", [H, S], BF16, **sk).ap()
    y_hbm = nc.dram_tensor("y", [H, S], F32, kind="ExternalOutput").ap()
    if debug:
        k_dbg = nc.dram_tensor("k_dbg", [H, S], BF16, kind="ExternalOutput").ap()
        v_dbg = nc.dram_tensor("v_dbg", [128, 32, 1040], BF16, kind="ExternalOutput").ap()
        wte_dbg = nc.dram_tensor("wte_dbg", [NE, S], BF16, kind="ExternalOutput").ap()
        cb_dbg = nc.dram_tensor("cb_dbg", [128, NH], F32, kind="ExternalOutput").ap()

    P = Prog()
    FMRE = "(c p) s -> p c s"

    with ExitStack() as eso:
        sems = [eso.enter_context(nc.semaphore(n))
                for n in ("ldg", "lds", "smm", "sac", "sev")]
        ldg, lds, smm, sac, sev = sems

        wte_fm = eso.enter_context(nc.sbuf_tensor("wte_fm", [NE, S], BF16))
        ones_bf = eso.enter_context(nc.sbuf_tensor("ones_bf", [128, 4], BF16))
        onesr_bf = eso.enter_context(nc.sbuf_tensor("onesr_bf", [1, 128], BF16))
        cbias = eso.enter_context(nc.sbuf_tensor("cbias", [128, NH], F32))

        P.do("gpsimd", lambda e: e.dma_start(out=ones_bf[:, :], in_=onesP), inc=ldg, incn=16)
        P.do("gpsimd", lambda e: e.dma_start(out=onesr_bf[:, :], in_=onesrP), inc=ldg, incn=16)
        lds_consts = P.cnt(lds)
        ldg_consts = P.cnt(ldg)

        with ExitStack() as eskv:
            Tk = lambda *a: eskv.enter_context(nc.sbuf_tensor(*a))
            K_res = Tk("K_res", [128, HC, S], BF16)
            V_res = Tk("V_res", [128, 32, 1040], BF16)

            precast = []
            for _e in range(NE):
                for _fs in range(NFS):
                    precast.append((w13B[_e, _fs], w13P[_e, _fs]))
            for _e in range(NE):
                for _q4 in range(4):
                    precast.append((w2B[_e, _q4 * 8:(_q4 + 1) * 8],
                                    w2P[_e, _q4 * 8:(_q4 + 1) * 8]))
            precast_i = [0]

            precast_cnt = {}

            def rec_precast(n, throttle=False):
                while n > 0 and precast_i[0] < len(precast):
                    i = precast_i[0]
                    dst, srcp = precast[i]
                    precast_i[0] += 1
                    w = []
                    if throttle and (i - 2) in precast_cnt:
                        w.append((ldg, precast_cnt[i - 2]))
                    P.do("gpsimd", lambda e, dst=dst, srcp=srcp: e.dma_start(
                        out=dst, in_=srcp), waits=w, inc=ldg, incn=16)
                    precast_cnt[i] = P.cnt(ldg)
                    n -= 1

            with ExitStack() as es1:
                T = lambda *a: es1.enter_context(nc.sbuf_tensor(*a))
                wqbuf = T("wqbuf", [128, 2, HC, 512], BF16)
                ident_sb = T("ident_sb", [128, 128], F32)
                xbf = T("xbf", [128, HC, SG], BF16)
                xf32 = T("xf32", [128, 2, SG], F32)
                sqb = T("sqb", [128, 2, SG], BF16)
                xn_grp = T("xn_grp", [128, HC, SG], BF16)
                xn2_grp = T("xn2_grp", [128, HC, SG], BF16)
                u_sb = T("u_sb", [128, SG], F32)
                AB_sb = T("AB_sb", [128, 2 * SG], F32)
                rowsA = T("rowsA", [1, SG * 3], F32)
                rowsB = T("rowsB", [1, SG * 2], BF16)
                qst = T("qst", [128, 2, SG], BF16)
                gate_sb = T("gate_sb", [128, HC, NE], BF16)
                bqkv_sb = T("bqkv_sb", [128, 24], F32)
                gm_sb = T("gm_sb", [128, HC], F32)
                bt_sb = T("bt_sb", [128, HC], F32)
                vb_sb = T("vb_sb", [128, H], BF16)
                bv_bf = T("bv_bf", [1, H], BF16)
                maxflat = T("maxflat", [1, 64], F32)
                rm_sb = T("rm_sb", [2, 1], F32)
                rt = {}
                for n in ("m1", "nm1", "sum1", "mult1", "m2", "nm2", "sum2", "mult2"):
                    rt[n] = T("rt_" + n, [128, 1], F32)
                for n in ("is1", "f1", "d1", "t1", "keep1", "ex1", "e1", "l2", "absl", "lgs",
                          "is2", "f2", "d2", "t2", "keep2", "ex2e", "e2", "wte_tm"):
                    rt[n] = T("rt_" + n, [128, NE], F32)

                with ExitStack() as ps1:
                    PT = lambda *a: ps1.enter_context(nc.psum_tensor(*a))
                    ps_stA = PT("ps_stA", [128, 512], F32)
                    ps_stB = PT("ps_stB", [128, 512], F32)
                    ps_ab = PT("ps_ab", [128, 1024], F32)
                    ps_rot = PT("ps_rot", [128, 1536], F32)
                    ps_rt = PT("ps_rt", [128, 512], F32)

                    P.do("gpsimd", lambda e: e.dma_start(out=gate_sb[:, :, :], in_=gateP.rearrange("c p e -> p c e")),
                         inc=ldg, incn=16)
                    P.do("gpsimd", lambda e: e.dma_start(out=bv_bf[:, :], in_=bvrow),
                         inc=ldg, incn=16)
                    ldg_small = P.cnt(ldg)
                    P.do("sync", lambda e: e.dma_start(out=ident_sb[:, :], in_=identP), inc=lds, incn=16)
                    P.do("sync", lambda e: e.dma_start(out=bqkv_sb[:, :], in_=bqkvP), inc=lds, incn=16)
                    P.do("sync", lambda e: e.dma_start(out=gm_sb[:, :], in_=gammaP), inc=lds, incn=16)
                    P.do("sync", lambda e: e.dma_start(out=bt_sb[:, :], in_=betaP), inc=lds, incn=16)
                    lds_small = P.cnt(lds)

                    P.do("vector", lambda e: e.memset(
                        V_res[:, :, :].rearrange("p k (h x) -> p k h x", x=65)[:, :, :, 64:65], 1.0),
                        inc=sev)
                    P.do("vector", lambda e: e.memset(maxflat[:, :], 0.0), inc=sev)

                    for half in range(2):
                        P.do("tensor", lambda e, h=half: e.matmul(
                            ps_ab[:, h * 512:(h + 1) * 512], onesr_bf[:, :],
                            bv_bf[:, h * 512:(h + 1) * 512], start=True, stop=True),
                            waits=[(ldg, max(ldg_small, ldg_consts))], inc=smm)
                    smm_vb = P.cnt(smm)
                    P.do("scalar", lambda e: e.activation(vb_sb[:, :], ps_ab[:, :], AF.Copy),
                         waits=[(smm, smm_vb)], inc=sac)
                    last_ab_reader = P.cnt(sac)
                    sac_vb = P.cnt(sac)

                    NSLAB = NSG * 6
                    ldg_slab = {}
                    smm_after_slab = {}

                    def rec_slab(k):
                        if k >= NSLAB:
                            return
                        j = k % 6
                        w = []
                        if k >= 2:
                            w.append((smm, smm_after_slab[k - 2]))
                        P.do("gpsimd", lambda e, j=j, k=k: e.dma_start(
                            out=wqbuf[:, k % 2], in_=wqkvP[j]), waits=w, inc=ldg, incn=16)
                        ldg_slab[k] = P.cnt(ldg)

                    rec_slab(0)
                    rec_slab(1)

                    ldg_xbf = {}
                    lds_xf = {}
                    lds_qstore = {}
                    sev_xn_done = {}
                    sev_xn2_done = {}
                    sev_evac_rot = {}
                    smm_norm = {}
                    sev_stab_end = {-1: 0}
                    sac_rt_end = {-1: 0}
                    sev_rt_end = {-1: 0}
                    sev_rt_last_of = {}
                    sac_wte = {}
                    sev_ab_read = 0
                    smm_stats_end = {-1: 0}
                    sac_sq_end = {-1: 0}
                    rot_i = [0]

                    stats_read2_of = {-1: 0}
                    smm_abcde_of = {}
                    smm_st2_of = {}
                    sac_B_end = {-1: 0}

                    def ln_rows(smm_done):
                        nonlocal last_ab_reader
                        mu = rowsA[:, 0:SG]
                        ex2 = rowsA[:, SG:2 * SG]
                        var = rowsA[:, 2 * SG:3 * SG]
                        Arow = rowsA[:, 2 * SG:3 * SG]
                        Brow = rowsA[:, 0:SG]
                        P.do("vector", lambda e: e.tensor_scalar(
                            mu, ps_stA[0:1, :], 1.0 / H, None, ALU.mult),
                            waits=[(smm, smm_done)], inc=sev)
                        P.do("vector", lambda e: e.tensor_scalar(
                            ex2, ps_stB[0:1, :], 1.0 / H, None, ALU.mult), inc=sev)
                        P.do("vector", lambda e: e.tensor_mul(var, mu, mu), inc=sev)
                        P.do("vector", lambda e: e.tensor_sub(var, ex2, var), inc=sev)
                        P.do("vector", lambda e: e.tensor_scalar(var, var, EPS, None, ALU.add),
                             inc=sev)
                        sev_v = P.cnt(sev)
                        stats_read = P.cnt(sev)
                        P.do("scalar", lambda e: e.activation(ex2, var, AF.Ln),
                             waits=[(sev, sev_v)], inc=sac)
                        P.do("scalar", lambda e: e.activation(Arow, ex2, AF.Exp, scale=-0.5),
                             inc=sac)
                        sac_a = P.cnt(sac)
                        P.do("vector", lambda e: e.tensor_mul(Brow, mu, Arow),
                             waits=[(sac, sac_a)], inc=sev)
                        P.do("vector", lambda e: e.tensor_copy(rowsB[:, 0:SG], Arow), inc=sev)
                        P.do("vector", lambda e: e.tensor_scalar(
                            rowsB[:, SG:2 * SG], Brow, -1.0, None, ALU.mult), inc=sev)
                        sev_rows = P.cnt(sev)
                        for h in range(2):
                            P.do("tensor", lambda e, h=h: e.matmul(
                                ps_ab[:, h * 512:(h + 1) * 512], onesr_bf[:, :],
                                rowsB[:, h * SG:(h + 1) * SG], start=True, stop=True),
                                waits=[(sev, sev_rows), (sac, last_ab_reader)], inc=smm)
                        smm_b = P.cnt(smm)
                        P.do("scalar", lambda e: e.activation(AB_sb[:, :], ps_ab[:, :], AF.Copy),
                             waits=[(smm, smm_b)], inc=sac)
                        last_ab_reader = P.cnt(sac)
                        return P.cnt(sac), stats_read


                    def recABCDE(g):
                        nonlocal sev_ab_read
                        smm_g0 = P.cnt(smm)
                        w = []
                        if g >= 1:
                            w.append((smm, smm_stats_end[g - 1]))
                            w.append((sac, sac_B_end[g - 1]))
                        P.do("gpsimd", lambda e, g=g: e.dma_start(
                            out=xbf[:, :, :],
                            in_=x_fm.rearrange(FMRE, p=128)[:, :, g * SG:(g + 1) * SG]),
                            waits=w, inc=ldg, incn=16)
                        ldg_xbf[g] = P.cnt(ldg)
                        def rec_xf(c, g=None):
                            gi = g * 8 + c
                            w = []
                            if gi >= 2:
                                pg, pc = divmod(gi - 2, 8)
                                w.append((sev, sev_xn_done[(pg, pc)]))
                            P.do("sync", lambda e, g=g, c=c: e.dma_start(
                                out=xf32[:, c % 2, :],
                                in_=x_fm[c * 128:(c + 1) * 128, g * SG:(g + 1) * SG]),
                                waits=w, inc=lds, incn=16)
                            lds_xf[(g, c)] = P.cnt(lds)

                        rec_xf(0, g)
                        rec_xf(1, g)

                        smm_sq_mm = {}
                        for c in range(HC):
                            w = [(ldg, ldg_xbf[g])]
                            if c >= 2:
                                w.append((smm, smm_sq_mm[c - 2]))
                            elif g >= 1:
                                w.append((smm, smm_g0))
                            P.do("scalar", lambda e, c=c: e.activation(
                                sqb[:, c % 2, :], xbf[:, c, :], AF.Square), waits=w, inc=sac)
                            sac_sq = P.cnt(sac)
                            w = [(ldg, max(ldg_xbf[g], ldg_consts))]
                            if c == 0 and g >= 1:
                                w.append((sev, stats_read2_of[g - 1]))
                            P.do("tensor", lambda e, c=c: e.matmul(
                                ps_stA[0:1, :], ones_bf[:, 0:1], xbf[:, c, :],
                                start=(c == 0), stop=(c == 7)), waits=w, inc=smm)
                            P.do("tensor", lambda e, c=c: e.matmul(
                                ps_stB[0:1, :], ones_bf[:, 0:1], sqb[:, c % 2, :],
                                start=(c == 0), stop=(c == 7)),
                                waits=[(sac, sac_sq)], inc=smm)
                            smm_sq_mm[c] = P.cnt(smm)
                        smm_st1 = P.cnt(smm)
                        sac_ab, sev_stats_read1 = ln_rows(smm_st1)

                        sac_B_end[g] = P.cnt(sac)
                        smm_sq2_mm = {}
                        for c in range(HC):
                            P.do("vector", lambda e, c=c: e.tensor_mul(
                                u_sb[:, :], xf32[:, c % 2, :], AB_sb[:, 0:SG]),
                                waits=[(lds, lds_xf[(g, c)]), (sac, sac_ab)], inc=sev)
                            P.do("vector", lambda e, c=c: e.tensor_add(
                                xn_grp[:, c, :], u_sb[:, :], AB_sb[:, SG:2 * SG]), inc=sev)
                            sev_xn_done[(g, c)] = P.cnt(sev)
                            if c + 2 < HC:
                                rec_xf(c + 2, g)
                            w = [(sev, sev_xn_done[(g, c)])]
                            if c < 2:
                                w.append((smm, smm_st1))
                            else:
                                w.append((smm, smm_sq2_mm[c - 2]))
                            P.do("scalar", lambda e, c=c: e.activation(
                                sqb[:, c % 2, :], xn_grp[:, c, :], AF.Square), waits=w, inc=sac)
                            sac_sq2 = P.cnt(sac)
                            w = [(sev, sev_xn_done[(g, c)])]
                            if c == 0:
                                w.append((sev, sev_stats_read1))
                            P.do("tensor", lambda e, c=c: e.matmul(
                                ps_stA[0:1, :], ones_bf[:, 0:1], xn_grp[:, c, :],
                                start=(c == 0), stop=(c == 7)), waits=w, inc=smm)
                            P.do("tensor", lambda e, c=c: e.matmul(
                                ps_stB[0:1, :], ones_bf[:, 0:1], sqb[:, c % 2, :],
                                start=(c == 0), stop=(c == 7)), waits=[(sac, sac_sq2)], inc=smm)
                            smm_sq2_mm[c] = P.cnt(smm)
                        smm_st2 = P.cnt(smm)
                        smm_stats_end[g] = P.cnt(smm)
                        P.do("sync", lambda e, g=g: e.dma_start(
                            out=xn_hbm.rearrange(FMRE, p=128)[:, :, g * SG:(g + 1) * SG],
                            in_=xn_grp[:, :, :]),
                            waits=[(sev, sev_xn_done[(g, 7)])], inc=lds, incn=16)

                        sac_ab2, sev_stats_read2 = ln_rows(smm_st2)
                        sac_ab2_of[g] = sac_ab2
                        stats_read2_of[g] = sev_stats_read2
                        smm_st2_of[g] = smm_st2
                        smm_abcde_of[g] = P.cnt(smm)

                    def recF(g):
                        nonlocal sev_ab_read
                        sac_ab2 = sac_ab2_of[g]
                        for c in range(HC):
                            P.do("vector", lambda e, c=c: e.tensor_mul(
                                u_sb[:, :], xn_grp[:, c, :], AB_sb[:, 0:SG]),
                                waits=[(sac, sac_ab2)], inc=sev)
                            P.do("vector", lambda e, c=c: e.tensor_add(
                                u_sb[:, :], u_sb[:, :], AB_sb[:, SG:2 * SG]), inc=sev)
                            P.do("vector", lambda e, c=c: e.scalar_tensor_tensor(
                                xn2_grp[:, c, :], u_sb[:, :], gm_sb[:, c:c + 1],
                                bass.AP(bt_sb, c, [[HC, 128], [0, SG]]),
                                ALU.mult, ALU.add),
                                waits=[(lds, lds_small)], inc=sev)
                            sev_xn2_done[(g, c)] = P.cnt(sev)
                        sev_ab_read = P.cnt(sev)


                    def recG(g):
                        nonlocal sev_rt_prev
                        for st in range(4):
                            w = [(sev, sev_xn_done[(g, 7)]), (ldg, ldg_small),
                                 (sev, sev_stab_end[g - 1]),
                                 (sac, sac_rt_end[g - 1] if st == 0 else sac_wte[(g, st - 1)]),
                                 (sev, sev_rt_end[g - 1] if st == 0 else 0)]
                            if st > 0:
                                w.append((sev, sev_rt_prev))
                            for c in range(HC):
                                P.do("tensor", lambda e, c=c, st=st: e.matmul(
                                    ps_rt[:, 0:NE], xn_grp[:, c, st * 128:(st + 1) * 128],
                                    gate_sb[:, c, :], start=(c == 0), stop=(c == 7)),
                                    waits=(w if c == 0 else []), inc=smm)
                            smm_lg = P.cnt(smm)
                            m1v = bass.AP(rt["m1"], 0, [[1, 128], [0, NE]])
                            m2v = bass.AP(rt["m2"], 0, [[1, 128], [0, NE]])
                            m1s = bass.AP(rt["mult1"], 0, [[1, 128], [0, NE]])
                            m2s = bass.AP(rt["mult2"], 0, [[1, 128], [0, NE]])

                            def V_(fn, ww=()):
                                P.do("vector", fn, waits=ww, inc=sev)

                            P.do("scalar", lambda e: e.activation(
                                rt["lgs"][:, :], ps_rt[:, 0:NE], AF.Copy),
                                waits=[(smm, smm_lg)], inc=sac)
                            P.do("scalar", lambda e: e.activation(
                                rt["absl"][:, :], ps_rt[:, 0:NE], AF.Abs),
                                waits=[(smm, smm_lg)], inc=sac)
                            sac_abs = P.cnt(sac)
                            V_(lambda e: e.reduce_max(rt["m1"][:, :], rt["lgs"][:, :], axis=AX.X),
                               [(sac, sac_abs)])
                            V_(lambda e: e.tensor_tensor(rt["is1"][:, :], rt["lgs"][:, :], m1v, ALU.is_equal))
                            V_(lambda e: e.tensor_tensor(rt["f1"][:, :], rt["absl"][:, :], m1v, ALU.max),
                               [(sac, sac_abs)])
                            V_(lambda e: e.tensor_tensor(rt["d1"][:, :], m1v, rt["lgs"][:, :], ALU.subtract))
                            V_(lambda e: e.tensor_scalar(rt["t1"][:, :], rt["f1"][:, :], JEPS, None, ALU.mult))
                            V_(lambda e: e.tensor_tensor(rt["keep1"][:, :], rt["d1"][:, :], rt["t1"][:, :], ALU.is_le))
                            V_(lambda e: e.tensor_scalar(rt["nm1"][:, :], rt["m1"][:, :], -1.0, None, ALU.mult))
                            sev_r1 = P.cnt(sev)
                            P.do("scalar", lambda e: e.activation(
                                rt["ex1"][:, :], rt["lgs"][:, :], AF.Exp, bias=rt["nm1"][:, :]),
                                waits=[(sev, sev_r1)], inc=sac)
                            sac_r1 = P.cnt(sac)
                            V_(lambda e: e.tensor_mul(rt["e1"][:, :], rt["keep1"][:, :], rt["ex1"][:, :]),
                               [(sac, sac_r1)])
                            V_(lambda e: e.reduce_sum(rt["sum1"][:, :], rt["e1"][:, :], axis=AX.X))
                            V_(lambda e: e.reciprocal(rt["mult1"][:, :], rt["sum1"][:, :]))
                            V_(lambda e: e.scalar_tensor_tensor(
                                rt["l2"][:, :], rt["is1"][:, :], -1e30, rt["lgs"][:, :], ALU.mult, ALU.add))
                            V_(lambda e: e.reduce_max(rt["m2"][:, :], rt["l2"][:, :], axis=AX.X))
                            V_(lambda e: e.tensor_tensor(rt["is2"][:, :], rt["l2"][:, :], m2v, ALU.is_equal))
                            V_(lambda e: e.tensor_tensor(rt["f2"][:, :], rt["absl"][:, :], m2v, ALU.max))
                            V_(lambda e: e.tensor_tensor(rt["d2"][:, :], m2v, rt["lgs"][:, :], ALU.subtract))
                            V_(lambda e: e.tensor_scalar(rt["t2"][:, :], rt["f2"][:, :], JEPS, None, ALU.mult))
                            V_(lambda e: e.tensor_tensor(rt["keep2"][:, :], rt["d2"][:, :], rt["t2"][:, :], ALU.is_le))
                            V_(lambda e: e.tensor_sub(rt["keep2"][:, :], rt["keep2"][:, :], rt["is1"][:, :]))
                            V_(lambda e: e.tensor_scalar(rt["nm2"][:, :], rt["m2"][:, :], -1.0, None, ALU.mult))
                            sev_r2 = P.cnt(sev)
                            P.do("scalar", lambda e: e.activation(
                                rt["ex2e"][:, :], rt["l2"][:, :], AF.Exp, bias=rt["nm2"][:, :]),
                                waits=[(sev, sev_r2)], inc=sac)
                            sac_r2 = P.cnt(sac)
                            V_(lambda e: e.tensor_mul(rt["e2"][:, :], rt["keep2"][:, :], rt["ex2e"][:, :]),
                               [(sac, sac_r2)])
                            V_(lambda e: e.reduce_sum(rt["sum2"][:, :], rt["e2"][:, :], axis=AX.X))
                            V_(lambda e: e.reciprocal(rt["mult2"][:, :], rt["sum2"][:, :]))
                            V_(lambda e: e.tensor_mul(rt["wte_tm"][:, :], rt["is1"][:, :], m1s))
                            V_(lambda e: e.scalar_tensor_tensor(
                                rt["e2"][:, :], rt["is2"][:, :], 1.0, m2s, ALU.mult, ALU.mult))
                            V_(lambda e: e.tensor_add(rt["wte_tm"][:, :], rt["wte_tm"][:, :], rt["e2"][:, :]))
                            sev_rt = P.cnt(sev)
                            sev_rt_prev = sev_rt
                            P.do("tensor", lambda e: e.transpose(
                                bass.AP(ps_rt, 128, [[512, NE], [1, 128]]),
                                rt["wte_tm"][:, :], ident_sb[:, :]),
                                waits=[(sev, sev_rt), (lds, lds_consts)], inc=smm)
                            smm_tr = P.cnt(smm)
                            P.do("scalar", lambda e, g=g, st=st: e.activation(
                                wte_fm[:, g * SG + st * 128: g * SG + (st + 1) * 128],
                                bass.AP(ps_rt, 128, [[512, NE], [1, 128]]), AF.Copy),
                                waits=[(smm, smm_tr)], inc=sac)
                            sac_wte[(g, st)] = P.cnt(sac)
                        sev_rt_last_of[g] = P.cnt(sev)
                        sac_rt_end[g] = P.cnt(sac)
                        sev_rt_end[g] = P.cnt(sev)


                    def recH(g):
                        first_norm = True
                        for j in range(6):
                            k = g * 6 + j
                            if j < 4:
                                for ol in range(4):
                                    oc = j * 4 + ol
                                    rot = rot_i[0] % 3
                                    rot_i[0] += 1
                                    pb = ps_rot[:, rot * 512:(rot + 1) * 512]
                                    w = [(ldg, ldg_slab[k]), (sev, sev_xn2_done[(g, 7)])]
                                    if rot in sev_evac_rot:
                                        w.append((sev, sev_evac_rot[rot]))
                                    for c in range(HC):
                                        P.do("tensor", lambda e, c=c, k=k, ol=ol, pb=pb: e.matmul(
                                            pb, wqbuf[:, k % 2, c, ol * 128:(ol + 1) * 128],
                                            xn2_grp[:, c, :], start=(c == 0), stop=(c == 7)),
                                            waits=(w if c == 0 else []), inc=smm)
                                    smm_oc = P.cnt(smm)
                                    smm_after_slab[k] = P.cnt(smm)
                                    bq = bass.AP(bqkv_sb, oc, [[24, 128], [0, SG]])
                                    w = [(smm, smm_oc), (lds, lds_small)]
                                    if oc < 8:
                                        dst = qst[:, oc % 2, :]
                                        if (g, oc - 2) in lds_qstore:
                                            w.append((lds, lds_qstore[(g, oc - 2)]))
                                        elif g >= 1 and (g - 1, 6 + oc) in lds_qstore:
                                            w.append((lds, lds_qstore[(g - 1, 6 + oc)]))
                                    else:
                                        dst = K_res[:, oc - 8, g * SG:(g + 1) * SG]
                                    P.do("vector", lambda e, pb=pb, dst=dst, bq=bq: e.tensor_tensor(
                                        dst, pb, bq, ALU.add), waits=w, inc=sev)
                                    sev_evac = P.cnt(sev)
                                    sev_evac_rot[rot] = sev_evac
                                    if oc < 8:
                                        P.do("sync", lambda e, g=g, oc=oc: e.dma_start(
                                            out=q_hbm[oc * 128:(oc + 1) * 128, g * SG:(g + 1) * SG],
                                            in_=qst[:, oc % 2, :]),
                                            waits=[(sev, sev_evac)], inc=lds, incn=16)
                                        lds_qstore[(g, oc)] = P.cnt(lds)
                                    w = [(sev, sev_evac)]
                                    if oc < 2:
                                        w.append((smm, smm_abcde_of.get(g + 1, smm_st2_of[g])))
                                    else:
                                        w.append((smm, smm_norm[(g, oc - 2)]))
                                    P.do("scalar", lambda e, dst=dst, oc=oc: e.activation(
                                        sqb[:, oc % 2, :], dst, AF.Square), waits=w, inc=sac)
                                    sac_sq = P.cnt(sac)
                                    w = [(sac, sac_sq)]
                                    if first_norm:
                                        w.append((sev, sev_rt_last_of[g]))
                                        w.append((sac, sac_wte[(g, 3)]))
                                        first_norm = False
                                    P.do("tensor", lambda e, oc=oc: e.matmul(
                                        ps_rt[0:2, 0:512], ones_bf[:, 1:3], sqb[:, oc % 2, :],
                                        start=True, stop=True), waits=w, inc=smm)
                                    smm_n = P.cnt(smm)
                                    P.do("vector", lambda e: e.reduce_max(
                                        rm_sb[:, :], ps_rt[0:2, 0:512], axis=AX.X),
                                        waits=[(smm, smm_n)], inc=sev)
                                    sev_rm = P.cnt(sev)
                                    P.do("tensor", lambda e: e.transpose(
                                        bass.AP(ps_rt, 384, [[512, 1], [1, 2]]),
                                        rm_sb[:, :], ident_sb[0:2, 0:2]),
                                        waits=[(sev, sev_rm)], inc=smm)
                                    smm_tr2 = P.cnt(smm)
                                    smm_norm[(g, oc)] = P.cnt(smm)
                                    P.do("vector", lambda e, oc=oc: e.tensor_tensor(
                                        maxflat[:, 2 * oc:2 * oc + 2], maxflat[:, 2 * oc:2 * oc + 2],
                                        bass.AP(ps_rt, 384, [[512, 1], [1, 2]]), ALU.max),
                                        waits=[(smm, smm_tr2)], inc=sev)
                            else:
                                for st in range(4):
                                    rot = rot_i[0] % 3
                                    rot_i[0] += 1
                                    pb = ps_rot[:, rot * 512:(rot + 1) * 512]
                                    w = [(ldg, ldg_slab[k]), (sev, sev_xn2_done[(g, 7)])]
                                    if rot in sev_evac_rot:
                                        w.append((sev, sev_evac_rot[rot]))
                                    for c in range(HC):
                                        P.do("tensor", lambda e, c=c, st=st, k=k, pb=pb: e.matmul(
                                            pb, xn2_grp[:, c, st * 128:(st + 1) * 128],
                                            wqbuf[:, k % 2, c, :], start=(c == 0), stop=(c == 7)),
                                            waits=(w if c == 0 else []), inc=smm)
                                    smm_v = P.cnt(smm)
                                    smm_after_slab[k] = P.cnt(smm)
                                    kt = g * 4 + st
                                    jj = j - 4
                                    dst = V_res[:, kt, jj * 520:(jj + 1) * 520].rearrange(
                                        "p (h x) -> p h x", x=65)[:, :, 0:64]
                                    P.do("vector", lambda e, pb=pb, dst=dst, jj=jj: e.tensor_tensor(
                                        dst, pb.rearrange("p (h x) -> p h x", x=64),
                                        vb_sb[:, jj * 512:(jj + 1) * 512].rearrange(
                                            "p (h x) -> p h x", x=64), ALU.add),
                                        waits=[(smm, smm_v), (sac, sac_vb)], inc=sev)
                                    sev_evac_rot[rot] = P.cnt(sev)
                            rec_slab(k + 2)
                        rec_precast(10)
                        sev_stab_end[g] = P.cnt(sev)
                        sac_sq_end[g] = P.cnt(sac)


                    sac_ab2_of = {}
                    sev_rt_prev = 0
                    recABCDE(0)
                    recF(0)
                    for g in range(NSG):
                        recG(g)
                        if g + 1 < NSG:
                            recABCDE(g + 1)
                        recH(g)
                        if g + 1 < NSG:
                            recF(g + 1)
                    sev_mf = P.cnt(sev)
                    P.do("vector", lambda e: e.tensor_mul(
                        maxflat[:, 32:48], maxflat[:, 0:16], maxflat[:, 16:32]),
                        waits=[(sev, sev_mf)], inc=sev)
                    sev_p = P.cnt(sev)
                    P.do("scalar", lambda e: e.activation(
                        maxflat[:, 48:64], maxflat[:, 32:48], AF.Sqrt),
                        waits=[(sev, sev_p)], inc=sac)
                    sac_s = P.cnt(sac)
                    P.do("vector", lambda e: e.tensor_scalar(
                        rowsB[:, 0:16], maxflat[:, 48:64], -0.130, None, ALU.mult),
                        waits=[(sac, sac_s)], inc=sev)
                    sev_c = P.cnt(sev)
                    P.do("tensor", lambda e: e.matmul(
                        ps_rt[:, 0:16], onesr_bf[:, :], rowsB[:, 0:16], start=True, stop=True),
                        waits=[(sev, sev_c)], inc=smm)
                    smm_cb = P.cnt(smm)
                    P.do("scalar", lambda e: e.activation(cbias[:, :], ps_rt[:, 0:16], AF.Copy),
                         waits=[(smm, smm_cb)], inc=sac)

                    if debug:
                        allw = P.snap(sems)
                        P.do("sync", lambda e: e.dma_start(
                            out=k_dbg.rearrange(FMRE, p=128), in_=K_res[:, :, :]),
                            waits=allw, inc=lds, incn=16)
                        P.do("sync", lambda e: e.dma_start(out=v_dbg, in_=V_res[:, :, :]),
                             waits=[], inc=lds, incn=16)
                        P.do("sync", lambda e: e.dma_start(out=wte_dbg, in_=wte_fm[:, :]),
                             waits=[], inc=lds, incn=16)
                        P.do("sync", lambda e: e.dma_start(out=cb_dbg, in_=cbias[:, :]),
                             waits=[], inc=lds, incn=16)

                    barrier1 = P.snap(sems)
                    P.emit(nc)

            if n_phases < 2:
                return nc

            with ExitStack() as es2:
                T2 = lambda *a: es2.enter_context(nc.sbuf_tensor(*a))
                woT = T2("woT", [128, HC, H], BF16)
                Qg = T2("Qg", [128, 2, HC, SG], BF16)
                xng = T2("xng", [128, 2, HC, SG], BF16)
                p_sb = T2("p_sb", [128, 2, 2, 1024], BF16)
                ctxg = T2("ctxg", [128, HC, SG], BF16)
                sst = T2("sst", [128, 2, SG], BF16)
                rbc = T2("rbc", [128, 2, SG], BF16)
                rec_sb = T2("rec_sb", [1, 2, SG], BF16)
                bout_sb = T2("bout_sb", [128, HC], F32)

                with ExitStack() as ps2:
                    PT2 = lambda *a: ps2.enter_context(nc.psum_tensor(*a))
                    ps_s = PT2("ps_s", [128, 2, 1024], F32)
                    ps_c = PT2("ps_c", [128, 2, 512], F32)
                    ps_o = PT2("ps_o", [128, 2, 512], F32)

                    P.do("gpsimd", lambda e: e.dma_start(
                        out=woT[:, :, :], in_=woutP.rearrange("c p o -> p c o")),
                        inc=ldg, incn=16)
                    ldg_wo = P.cnt(ldg)
                    P.do("sync", lambda e: e.dma_start(out=bout_sb[:, :], in_=boutP),
                         inc=lds, incn=16)
                    lds_bo = P.cnt(lds)

                    lds_q = {}
                    smm_qread = {}
                    sev_sst = {}
                    lds_sstore = {}

                    def rec_qload(q):
                        if q >= NSG:
                            return
                        w = []
                        if q >= 2:
                            w.append((smm, smm_qread[q - 2]))
                            w.append((sev, sev_sst.get((q - 2, HC - 1), 0)))
                        P.do("sync", lambda e, q=q: e.dma_start(
                            out=Qg[:, q % 2, :, :],
                            in_=q_hbm.rearrange(FMRE, p=128)[:, :, q * SG:(q + 1) * SG]),
                            waits=w, inc=lds, incn=16)
                        P.do("sync", lambda e, q=q: e.dma_start(
                            out=xng[:, q % 2, :, :],
                            in_=xn_hbm.rearrange(FMRE, p=128)[:, :, q * SG:(q + 1) * SG]),
                            waits=[], inc=lds, incn=16)
                        lds_q[q] = P.cnt(lds)

                    rec_qload(0)
                    rec_qload(1)

                    sac_exp = {}
                    smm_ctx_k = {}
                    sev_norm = {}
                    sac_norm = {}
                    sev_opev = {}

                    for q in range(NSG):
                        for hp in range(8):
                            def rec_scores(kt2, hp=hp, q=q):
                                w0 = [(lds, lds_q[q])]
                                for side in range(2):
                                    h = 2 * hp + side
                                    if kt2 >= 1:
                                        pk = sac_exp.get((q, h, kt2 - 1), 0)
                                    else:
                                        pk = sac_exp.get((q, h - 2, 15),
                                                         sac_exp.get((q - 1, NH - 2 + h, 15), 0))
                                    w = list(w0)
                                    if pk:
                                        w.append((sac, pk))
                                    if q >= 1 and hp == 0 and kt2 < 2:
                                        pj = sev_opev.get((q - 1, 6 + side), 0)
                                        if pj:
                                            w.append((sev, pj))
                                    base = 64 * side
                                    for half in range(2):
                                        kt = kt2 * 2 + half
                                        P.do("tensor", lambda e, hp=hp, kt=kt, base=base,
                                             side=side, half=half, q=q, kt2=kt2: e.matmul(
                                                 ps_s[:, side, half * 512:(half + 1) * 512],
                                                 K_res[base:base + 64, hp, kt * 128:(kt + 1) * 128],
                                                 Qg[base:base + 64, q % 2, hp, :],
                                                 start=True, stop=True, tile_position=(base, 0)),
                                            waits=(w if half == 0 else []), inc=smm)
                                    smm_sc = P.cnt(smm)
                                    h = 2 * hp + side
                                    w = [(smm, smm_sc)]
                                    pc = smm_ctx_k.get((q, h, kt2 - 2), 0)
                                    if pc:
                                        w.append((smm, pc))
                                    P.do("scalar", lambda e, side=side, kt2=kt2, h=h: e.activation(
                                        p_sb[:, side, kt2 % 2, :], ps_s[:, side, :], AF.Exp,
                                        bias=cbias[:, h:h + 1], scale=0.125),
                                        waits=w, inc=sac)
                                    sac_exp[(q, h, kt2)] = P.cnt(sac)

                            def rec_ctx(kt2, hp=hp, q=q):
                                for side in range(2):
                                    h = 2 * hp + side
                                    wc = [(sac, sac_exp[(q, h, kt2)])]
                                    if kt2 == 0:
                                        pn = sev_norm.get((q, h - 2), sev_norm.get((q - 1, NH - 2 + h), 0))
                                        if pn:
                                            wc.append((sev, pn))
                                    for half in range(2):
                                        kt = kt2 * 2 + half
                                        P.do("tensor", lambda e, h=h, kt=kt, kt2=kt2,
                                             side=side, half=half: e.matmul(
                                                 ps_c[0:65, side, :],
                                                 V_res[:, kt, h * 65:(h + 1) * 65],
                                                 p_sb[:, side, kt2 % 2, half * 512:(half + 1) * 512],
                                                 start=(kt == 0), stop=(kt == 31)),
                                            waits=(wc if half == 0 else []), inc=smm)
                                    smm_ctx_k[(q, h, kt2)] = P.cnt(smm)

                            rec_scores(0)
                            for kt2 in range(1, 16):
                                rec_scores(kt2)
                                rec_ctx(kt2 - 1)
                            rec_ctx(15)
                            smm_ctx = P.cnt(smm)
                            for side in range(2):
                                h = 2 * hp + side
                                base = 64 * side

                                def _recip(e, side=side):
                                    with nc.allow_low_precision(
                                            reason="softmax recip feeds bf16 matmul"):
                                        return e.reciprocal(rec_sb[:, side, :],
                                                            ps_c[64:65, side, :])
                                P.do("vector", _recip, waits=[(smm, smm_ctx)], inc=sev)
                                sev_rc = P.cnt(sev)
                                wb = [(sev, sev_rc), (ldg, ldg_consts)]
                                pn = sac_norm.get((q, h - 2), sac_norm.get((q - 1, NH - 2 + h), 0))
                                if pn:
                                    wb.append((sac, pn))
                                pnm = sev_norm.get((q, h - 2), sev_norm.get((q - 1, NH - 2 + h), 0))
                                if pnm:
                                    wb.append((sev, pnm))
                                po = sev_opev.get((q - 1, 6 + side), 0)
                                if hp == 0 and po:
                                    wb.append((sev, po))
                                P.do("tensor", lambda e, side=side: e.matmul(
                                    ps_o[:, side, :], onesr_bf[:, :], rec_sb[:, side, :],
                                    start=True, stop=True), waits=wb, inc=smm)
                                smm_bc = P.cnt(smm)
                                P.do("scalar", lambda e, side=side: e.activation(
                                    rbc[:, side, :], ps_o[:, side, :], AF.Copy),
                                    waits=[(smm, smm_bc)], inc=sac)
                                sac_bc = P.cnt(sac)
                                sac_norm[(q, h)] = sac_bc
                                P.do("vector", lambda e, side=side, hp=hp, base=base: e.tensor_mul(
                                    ctxg[base:base + 64, hp, :],
                                    ps_c[0:64, side, :], rbc[base:base + 64, side, :]),
                                    waits=[(sac, sac_bc)], inc=sev)
                                sev_norm[(q, h)] = P.cnt(sev)
                        smm_qread[q] = P.cnt(smm)
                        sev_ctx_all = P.cnt(sev)
                        for oc in range(HC):
                            pb = ps_o[:, oc % 2, :]
                            w = [(sev, sev_ctx_all), (ldg, ldg_wo), (lds, lds_q[q])]
                            pp = sev_sst.get((q, oc - 2), 0)
                            if pp:
                                w.append((sev, pp))
                            if oc < 2:
                                w.append((sac, sac_norm[(q, NH - 2 + oc)]))
                                w.append((sev, sev_norm[(q, NH - 2 + oc)]))
                            for c in range(HC):
                                P.do("tensor", lambda e, c=c, oc=oc, pb=pb: e.matmul(
                                    pb, woT[:, c, oc * 128:(oc + 1) * 128], ctxg[:, c, :],
                                    start=(c == 0), stop=(c == 7)),
                                    waits=(w if c == 0 else []), inc=smm)
                            smm_o = P.cnt(smm)
                            w = [(smm, smm_o), (lds, lds_bo)]
                            pst = lds_sstore.get((q, oc - 2), lds_sstore.get((q - 1, 6 + oc), 0))
                            if pst:
                                w.append((lds, pst))
                            P.do("vector", lambda e, oc=oc, pb=pb, q=q: e.scalar_tensor_tensor(
                                sst[:, oc % 2, :], pb, bout_sb[:, oc:oc + 1],
                                xng[:, q % 2, oc, :], ALU.add, ALU.add), waits=w, inc=sev)
                            sev_sst[(q, oc)] = P.cnt(sev)
                            sev_opev[(q, 6 + oc % 2)] = P.cnt(sev)
                            P.do("sync", lambda e, oc=oc, q=q: e.dma_start(
                                out=s_hbm[oc * 128:(oc + 1) * 128, q * SG:(q + 1) * SG],
                                in_=sst[:, oc % 2, :]),
                                waits=[(sev, sev_sst[(q, oc)])], inc=lds, incn=16)
                            lds_sstore[(q, oc)] = P.cnt(lds)
                        rec_precast(10, throttle=True)
                        rec_qload(q + 2)

                    rec_precast(1000, throttle=True)
                    barrier2 = P.snap(sems)
                    P.emit(nc, barrier_waits=barrier1)

        if n_phases < 3:
            return nc

        NCAP = 384
        NJC = NCAP // 128
        with ExitStack() as es3:
            T3 = lambda *a: es3.enter_context(nc.sbuf_tensor(*a))
            xn_tm = T3("xn_tm", [128, 32, 1024], BF16)
            h_c = T3("h_c", [128, 32, NCAP], BF16)
            acc = T3("acc", [128, HC, MG], F32)
            w13buf = T3("w13buf", [128, 2, 2, HC, FS], BF16)
            w2buf = T3("w2buf", [128, 2, 2, 1024], BF16)
            xgat = T3("xgat", [128, HC, NCAP], BF16)
            outT = T3("outT", [128, NJC, 1024], BF16)
            SelG = T3("SelG", [128, 8, NCAP], BF16)
            Scat = T3("Scat", [128, NJC, MG], BF16)
            poss = T3("poss", [8, 2, MG], F32)
            maskr = T3("maskr", [8, MG], F32)
            poss_tm = T3("poss_tm", [128, 8, 8], F32)
            poss_bc = T3("poss_bc", [128, MG], F32)
            wte_bc = T3("wte_bc", [128, MG], BF16)
            iseqT = T3("iseqT", [128, MG], BF16)
            jio_sb = T3("jio_sb", [128, NCAP], F32)
            pio_sb = T3("pio_sb", [128, 1], F32)
            pjc_sb = T3("pjc_sb", [128, 1], F32)
            sel_f = T3("sel_f", [NE, NE * 128], F32)
            sel_bf2 = T3("sel_bf2", [NE, NE * 128], BF16)
            id3 = T3("id3", [128, 128], F32)
            silu_sb3 = T3("silu_sb3", [128, 2, NCAP], BF16)

            with ExitStack() as ps3:
                ps_all = ps3.enter_context(nc.psum_tensor("ps_all", [128, 4096], F32))
                B512 = lambda b: ps_all[:, b * 512:(b + 1) * 512]
                BR = lambda b, n: ps_all[:, b * 512:(b + n) * 512]

                P.do("gpsimd", lambda e: e.dma_start(out=sel_bf2[:, :], in_=selP),
                     inc=ldg, incn=16)
                ldg_sel3 = P.cnt(ldg)
                P.do("sync", lambda e: e.dma_start(out=sel_f[:, :], in_=selP), inc=lds, incn=16)
                P.do("sync", lambda e: e.dma_start(out=jio_sb[:, :], in_=jiotaP), inc=lds, incn=16)
                P.do("sync", lambda e: e.dma_start(out=pio_sb[:, :], in_=piotaP), inc=lds, incn=16)
                P.do("sync", lambda e: e.dma_start(out=id3[:, :], in_=identP), inc=lds, incn=16)
                for t in range(32):
                    P.do("sync", lambda e, t=t: e.dma_start(
                        out=xn_tm[:, t, :], in_=xn_hbm[:, t * 128:(t + 1) * 128],
                        transpose=True), inc=lds, incn=16)
                lds_xtm = P.cnt(lds)
                lds_c3 = P.cnt(lds)

                NW13 = NMG * NE * NFS
                ldg_13 = {}
                lds_2 = {}
                smm_13_done = {}
                smm_2_done = {}

                def rec_w13(kk):
                    if kk >= NW13:
                        return
                    gi, rem = divmod(kk, NE * NFS)
                    ei, fs = divmod(rem, NFS)
                    w = []
                    if kk >= 2:
                        w.append((smm, smm_13_done.get(kk - 2, 0)))
                    P.do("gpsimd", lambda e, ei=ei, fs=fs, kk=kk: e.dma_start(
                        out=w13buf[:, :, kk % 2], in_=w13B[ei, fs]),
                        waits=w, inc=ldg, incn=16)
                    ldg_13[kk] = P.cnt(ldg)

                def rec_w2(kk):
                    if kk >= NMG * NE * 16:
                        return
                    gi, rem = divmod(kk, NE * 16)
                    ei, f2 = divmod(rem, 16)
                    w = []
                    if kk >= 2:
                        w.append((smm, smm_2_done.get(kk - 2, 0)))
                    P.do("sync", lambda e, ei=ei, f2=f2, kk=kk: e.dma_start(
                        out=w2buf[:, kk % 2], in_=w2B[ei, f2 * 2:f2 * 2 + 2].rearrange(
                            "f p o -> p f o")), waits=w, inc=lds, incn=16)
                    lds_2[kk] = P.cnt(lds)

                rec_w13(0)
                rec_w13(1)

                sev_h = {}
                sac_silu = {}
                prev_gather_smm = [0]
                prev_scatter_smm = [0]

                for gi in range(NMG):
                    sac_gi_start = P.cnt(sac)
                    gsl = slice(gi * MG, (gi + 1) * MG)
                    P.do("vector", lambda e, gsl=gsl: e.tensor_scalar(
                        maskr[:, :], wte_fm[0:NE, gsl], 0.0, None, ALU.is_gt), inc=sev)
                    cur = 0
                    P.do("vector", lambda e: e.tensor_copy(poss[:, 0, :], maskr[:, :]), inc=sev)
                    sh = 1
                    while sh < MG:
                        P.do("vector", lambda e, cur=cur, sh=sh: e.tensor_copy(
                            poss[:, 1 - cur, 0:sh], poss[:, cur, 0:sh]), inc=sev)
                        P.do("vector", lambda e, cur=cur, sh=sh: e.tensor_add(
                            poss[:, 1 - cur, sh:MG], poss[:, cur, sh:MG], poss[:, cur, 0:MG - sh]),
                            inc=sev)
                        cur = 1 - cur
                        sh *= 2
                    oth = 1 - cur
                    P.do("vector", lambda e, cur=cur: e.tensor_scalar(
                        poss[:, cur, :], poss[:, cur, :], -1.0, None, ALU.add), inc=sev)
                    P.do("vector", lambda e, cur=cur, oth=oth: e.tensor_mul(
                        poss[:, oth, :], poss[:, cur, :], maskr[:, :]), inc=sev)
                    P.do("vector", lambda e: e.tensor_scalar(
                        maskr[:, :], maskr[:, :], -4096.0, 4096.0, ALU.mult, ALU.add), inc=sev)
                    P.do("vector", lambda e, oth=oth: e.tensor_add(
                        poss[:, oth, :], poss[:, oth, :], maskr[:, :]), inc=sev)
                    sev_poss = P.cnt(sev)
                    for tc in range(8):
                        w = [(sev, sev_poss), (lds, lds_c3)]
                        if gi > 0 or tc > 0:
                            w.append((sev, P.cnt(sev)))
                        P.do("tensor", lambda e, tc=tc, oth=oth: e.transpose(
                            bass.AP(ps_all, 6 * 512, [[4096, 128], [1, 8]]),
                            poss[:, oth, tc * 128:(tc + 1) * 128], id3[0:8, 0:8]),
                            waits=w, inc=smm)
                        smm_tr = P.cnt(smm)
                        P.do("vector", lambda e, tc=tc: e.tensor_copy(
                            poss_tm[:, tc, :], bass.AP(ps_all, 6 * 512, [[4096, 128], [1, 8]])),
                            waits=[(smm, smm_tr)], inc=sev)
                    sev_ptm = P.cnt(sev)

                    state = {}

                    def rec_head(ei):
                        sev_e_start = P.cnt(sev)
                        sac_e_start = P.cnt(sac)
                        oth = state["oth"]
                        gsl = state["gsl"]
                        gi = state["gi"]
                        bb = 6 if ei == 0 else 0
                        for half in range(2):
                            P.do("tensor", lambda e, ei=ei, half=half, oth=oth, bb=bb: e.matmul(
                                B512(bb + half), sel_f[:, ei * 128:(ei + 1) * 128],
                                poss[:, oth, half * 512:(half + 1) * 512],
                                start=True, stop=True),
                                waits=[(sev, sev_e_start), (sac, sac_e_start),
                                       (lds, lds_c3)] if half == 0 else [],
                                inc=smm)
                        smm_pb = P.cnt(smm)
                        P.do("scalar", lambda e, bb=bb: e.activation(
                            poss_bc[:, :], BR(bb, 2), AF.Copy),
                            waits=[(smm, smm_pb)], inc=sac)
                        sac_pb = P.cnt(sac)
                        for half in range(2):
                            P.do("tensor", lambda e, ei=ei, half=half, gsl=gsl, bb=bb: e.matmul(
                                B512(bb + half), sel_bf2[:, ei * 128:(ei + 1) * 128],
                                wte_fm[0:NE, gsl][:, half * 512:(half + 1) * 512],
                                start=True, stop=True),
                                waits=[(sac, sac_pb), (ldg, ldg_sel3)] if half == 0 else [],
                                inc=smm)
                        smm_wb = P.cnt(smm)
                        P.do("scalar", lambda e, bb=bb: e.activation(
                            wte_bc[:, :], BR(bb, 2), AF.Copy),
                            waits=[(smm, smm_wb)], inc=sac)
                        sac_wb = P.cnt(sac)
                        for tc in range(8):
                            pv = bass.AP(poss_tm, tc * 8 + ei, [[64, 128], [0, NCAP]])
                            P.do("vector", lambda e, tc=tc, pv=pv: e.tensor_tensor(
                                SelG[:, tc, :], jio_sb[:, :], pv, ALU.is_equal),
                                waits=[(sev, state["sev_ptm"]), (smm, prev_gather_smm[0])],
                                inc=sev)
                        sev_selg = P.cnt(sev)
                        sac_xg = {}
                        for hc in range(HC):
                            rot = hc % 2
                            w = [(lds, lds_xtm), (sev, sev_selg)]
                            pg = sac_xg.get(hc - 2, 0)
                            if pg:
                                w.append((sac, pg))
                            if hc < 2:
                                w.append((sac, sac_wb))
                            for tc in range(8):
                                P.do("tensor", lambda e, hc=hc, tc=tc, rot=rot, gi=gi: e.matmul(
                                    ps_all[:, rot * 512:rot * 512 + NCAP],
                                    xn_tm[:, gi * 8 + tc, hc * 128:(hc + 1) * 128],
                                    SelG[:, tc, :], start=(tc == 0), stop=(tc == 7)),
                                    waits=(w if tc == 0 else []), inc=smm)
                            smm_g = P.cnt(smm)
                            P.do("scalar", lambda e, hc=hc, rot=rot: e.activation(
                                xgat[:, hc, :], ps_all[:, rot * 512:rot * 512 + NCAP], AF.Copy),
                                waits=[(smm, smm_g)], inc=sac)
                            sac_xg[hc] = P.cnt(sac)
                        state["sac_xg_all", ei] = P.cnt(sac)
                        state["sac_wb", ei] = sac_wb
                        state["sac_pb", ei] = sac_pb
                        prev_gather_smm[0] = P.cnt(smm)

                    def rec_mid(ei):
                        gi = state["gi"]
                        sac_xg_all = state["sac_xg_all", ei]
                        sev_e_start = P.cnt(sev)
                        kk0 = (gi * NE + ei) * NFS
                        for fs in range(NFS):
                            kk = kk0 + fs
                            for fcl in range(2):
                                fc = fs * 2 + fcl
                                jj = kk * 2 + fcl
                                pa = B512(2 + jj % 2)
                                pb = B512(4 + jj % 2)
                                wA = [(ldg, ldg_13[kk]), (sac, sac_xg_all)]
                                pv = sev_h.get(jj - 2, 0)
                                if pv:
                                    wA.append((sev, pv))
                                if fs == 0:
                                    wA.append((sev, sev_e_start))
                                for c in range(HC):
                                    P.do("tensor", lambda e, c=c, kk=kk, fcl=fcl, pa=pa: e.matmul(
                                        pa[:, 0:NCAP],
                                        w13buf[:, 0, kk % 2, c, fcl * 128:(fcl + 1) * 128],
                                        xgat[:, c, :], start=(c == 0), stop=(c == 7)),
                                        waits=(wA if c == 0 else []), inc=smm)
                                smm_a = P.cnt(smm)
                                wB = []
                                if pv:
                                    wB.append((sev, pv))
                                if fs == 0:
                                    wB.append((sev, sev_e_start))
                                for c in range(HC):
                                    P.do("tensor", lambda e, c=c, kk=kk, fcl=fcl, pb=pb: e.matmul(
                                        pb[:, 0:NCAP],
                                        w13buf[:, 1, kk % 2, c, fcl * 128:(fcl + 1) * 128],
                                        xgat[:, c, :], start=(c == 0), stop=(c == 7)),
                                        waits=(wB if c == 0 else []), inc=smm)
                                smm_b = P.cnt(smm)
                                smm_13_done[kk] = P.cnt(smm)
                                w = [(smm, smm_a)]
                                pv5 = sev_h.get(jj - 2, 0)
                                if pv5:
                                    w.append((sev, pv5))
                                P.do("scalar", lambda e, pa=pa, jj=jj: e.activation(
                                    silu_sb3[:, jj % 2, :], pa[:, 0:NCAP], AF.Silu),
                                    waits=w, inc=sac)
                                sac_silu[jj] = P.cnt(sac)
                                P.do("vector", lambda e, pb=pb, jj=jj, fc=fc: e.tensor_mul(
                                    h_c[:, fc, :], silu_sb3[:, jj % 2, :], pb[:, 0:NCAP]),
                                    waits=[(sac, sac_silu[jj]), (smm, smm_b)], inc=sev)
                                sev_h[jj] = P.cnt(sev)
                            rec_w13(kk + 2)
                        sev_h_all = P.cnt(sev)

                        kk20 = (gi * NE + ei) * 16
                        rec_w2(kk20)
                        rec_w2(kk20 + 1)
                        for f2 in range(16):
                            kk = kk20 + f2
                            w0 = [(lds, lds_2[kk]), (sev, sev_h_all), (sac, sac_xg_all)]
                            for fl in range(2):
                                fc = f2 * 2 + fl
                                for jc in range(NJC):
                                    for oh in range(2):
                                        P.do("tensor", lambda e, fc=fc, fl=fl, jc=jc, oh=oh, kk=kk: e.matmul(
                                            ps_all[:, (jc * 2 + oh) * 512:(jc * 2 + oh + 1) * 512],
                                            h_c[:, fc, jc * 128:(jc + 1) * 128],
                                            w2buf[:, kk % 2, fl, oh * 512:(oh + 1) * 512],
                                            start=(fc == 0), stop=(fc == 31)),
                                            waits=(w0 if (fl == 0 and jc == 0 and oh == 0) else []), inc=smm)
                            smm_2_done[kk] = P.cnt(smm)
                            rec_w2(kk + 2)
                        smm_w2_all = P.cnt(smm)
                        for jc in range(NJC):
                            P.do("scalar", lambda e, jc=jc: e.activation(
                                outT[:, jc, :], BR(jc * 2, 2), AF.Copy),
                                waits=[(smm, smm_w2_all)], inc=sac)
                        state["sac_oT_all", ei] = P.cnt(sac)
                        for jc in range(NJC):
                            P.do("vector", lambda e, jc=jc: e.tensor_scalar(
                                pjc_sb[:, :], pio_sb[:, :], float(jc * 128), None, ALU.add),
                                waits=[(lds, lds_c3)], inc=sev)
                            pv = bass.AP(pjc_sb, 0, [[1, 128], [0, MG]])
                            P.do("vector", lambda e, pv=pv: e.tensor_tensor(
                                iseqT[:, :], poss_bc[:, :], pv, ALU.is_equal),
                                waits=[(sac, state["sac_pb", ei])], inc=sev)
                            P.do("vector", lambda e, jc=jc: e.tensor_mul(
                                Scat[:, jc, :], iseqT[:, :], wte_bc[:, :]),
                                waits=[(sac, state["sac_wb", ei]), (smm, prev_scatter_smm[0])], inc=sev)
                        state["sev_scat_all", ei] = P.cnt(sev)

                    def rec_scatter(ei):
                        gi = state["gi"]
                        sev_scat_all = state["sev_scat_all", ei]
                        sac_oT_all = state["sac_oT_all", ei]
                        sev_aevac = {}
                        for oc in range(HC):
                            for th in range(2):
                                rot = (oc * 2 + th) % 2
                                w = [(sev, sev_scat_all), (sac, sac_oT_all)]
                                pe_ = sev_aevac.get((oc * 2 + th) - 2, 0)
                                if pe_:
                                    w.append((sev, pe_))
                                for jc in range(NJC):
                                    P.do("tensor", lambda e, oc=oc, th=th, jc=jc, rot=rot: e.matmul(
                                        B512(6 + rot),
                                        outT[:, jc, oc * 128:(oc + 1) * 128],
                                        Scat[:, jc, th * 512:(th + 1) * 512],
                                        start=(jc == 0), stop=(jc == NJC - 1)),
                                        waits=(w if jc == 0 else []), inc=smm)
                                smm_sc2 = P.cnt(smm)
                                wv = [(smm, smm_sc2)]
                                if ei == 0 and state["gi"] >= 1:
                                    wv.append((sac, state["sac_gi_start"]))
                                if ei == 0:
                                    P.do("vector", lambda e, oc=oc, th=th, rot=rot: e.tensor_copy(
                                        acc[:, oc, th * 512:(th + 1) * 512], B512(6 + rot)),
                                        waits=wv, inc=sev)
                                else:
                                    P.do("vector", lambda e, oc=oc, th=th, rot=rot: e.tensor_add(
                                        acc[:, oc, th * 512:(th + 1) * 512],
                                        acc[:, oc, th * 512:(th + 1) * 512], B512(6 + rot)),
                                        waits=wv, inc=sev)
                                sev_aevac[oc * 2 + th] = P.cnt(sev)
                        prev_scatter_smm[0] = P.cnt(smm)

                    state["gi"] = gi
                    state["oth"] = oth
                    state["gsl"] = gsl
                    state["sev_ptm"] = sev_ptm
                    state["sac_gi_start"] = sac_gi_start
                    rec_head(0)
                    for ei in range(NE):
                        rec_mid(ei)
                        if ei + 1 < NE:
                            rec_head(ei + 1)
                        rec_scatter(ei)
                    sev_acc_all = P.cnt(sev)
                    g_view = bass.AP(xgat, 0, [[HC * NCAP, 128], [1, MG]])
                    for oc in range(HC):
                        P.do("scalar", lambda e, oc=oc: e.activation(
                            g_view, acc[:, oc, :], AF.Silu),
                            waits=[(sev, sev_acc_all), (lds, P.cnt(lds))], inc=sac)
                        sac_g = P.cnt(sac)
                        P.do("sync", lambda e, oc=oc, gi=gi: e.dma_start(
                            out=g_hbm[oc * 128:(oc + 1) * 128, gi * MG:(gi + 1) * MG],
                            in_=g_view), waits=[(sac, sac_g)], inc=lds, incn=16)

                barrier3 = P.snap(sems)
                P.emit(nc, barrier_waits=barrier2)

        if n_phases < 4:
            return nc

        with ExitStack() as es4:
            T4 = lambda *a: es4.enter_context(nc.sbuf_tensor(*a))
            xt = T4("xt", [128, 2, S], F32)
            st4 = T4("st4", [128, 2, S], BF16)
            gt4 = T4("gt4", [128, 2, S], BF16)
            tt4 = T4("tt4", [128, S], F32)
            yt4 = T4("yt4", [128, 2, S], F32)

            sev_y = {}
            lds_tile = {}
            lds_yst = {}
            for i in range(HC):
                w = []
                if i >= 2:
                    w.append((sev, sev_y[i - 2]))
                P.do("sync", lambda e, i=i: e.dma_start(
                    out=xt[:, i % 2, :], in_=x_fm[i * 128:(i + 1) * 128, :]),
                    waits=w, inc=lds, incn=16)
                P.do("sync", lambda e, i=i: e.dma_start(
                    out=st4[:, i % 2, :], in_=s_hbm[i * 128:(i + 1) * 128, :]),
                    waits=[], inc=lds, incn=16)
                P.do("sync", lambda e, i=i: e.dma_start(
                    out=gt4[:, i % 2, :], in_=g_hbm[i * 128:(i + 1) * 128, :]),
                    waits=[], inc=lds, incn=16)
                lds_tile[i] = P.cnt(lds)
                P.do("vector", lambda e, i=i: e.tensor_mul(
                    tt4[:, :], st4[:, i % 2, :], gt4[:, i % 2, :]),
                    waits=[(lds, lds_tile[i])], inc=sev)
                w = []
                if i >= 2 and lds_yst.get(i - 2, 0):
                    w.append((lds, lds_yst[i - 2]))
                P.do("vector", lambda e, i=i: e.tensor_add(
                    yt4[:, i % 2, :], xt[:, i % 2, :], tt4[:, :]), waits=w, inc=sev)
                sev_y[i] = P.cnt(sev)
                P.do("sync", lambda e, i=i: e.dma_start(
                    out=y_hbm[i * 128:(i + 1) * 128, :],
                    in_=yt4[:, i % 2, :]), waits=[(sev, sev_y[i])], inc=lds, incn=16)
                lds_yst[i] = P.cnt(lds)

            P.do("sync", lambda e: e.nop(), waits=[(lds, P.cnt(lds))])
            P.emit(nc, barrier_waits=barrier3)

    return nc


def _pack(inputs):
    x = np.ascontiguousarray(inputs["L0_attn_hidden_states"])
    f32 = np.float32
    d = {}
    d["wqkvP"] = np.ascontiguousarray(
        inputs["L0_attn_W_qkv"].T.reshape(HC, 128, 6, 512).transpose(2, 1, 0, 3), dtype=f32)
    d["woutP"] = np.ascontiguousarray(
        inputs["L0_attn_W_out"].T.reshape(HC, 128, H), dtype=f32)
    w1p = inputs["L0_ffn_w1"].transpose(0, 2, 1).reshape(NE, HC, 128, NFS, FS).transpose(0, 3, 2, 1, 4)
    w3p = inputs["L0_ffn_w3"].transpose(0, 2, 1).reshape(NE, HC, 128, NFS, FS).transpose(0, 3, 2, 1, 4)
    d["w13P"] = np.ascontiguousarray(
        np.stack([w1p, w3p], axis=3), dtype=f32)
    d["w2P"] = np.ascontiguousarray(
        inputs["L0_ffn_w2"].transpose(0, 2, 1).reshape(NE, 32, 128, H), dtype=f32)
    d["gateP"] = np.ascontiguousarray(
        inputs["L0_ffn_gate_w"].T.reshape(HC, 128, NE), dtype=f32)
    d["bqkvP"] = np.ascontiguousarray(inputs["L0_attn_b_qkv"].reshape(24, 128).T, dtype=f32)
    d["bvrow"] = np.ascontiguousarray(inputs["L0_attn_b_qkv"][2048:].reshape(1, H), dtype=f32)
    d["boutP"] = np.ascontiguousarray(inputs["L0_attn_b_out"].reshape(HC, 128).T, dtype=f32)
    d["gammaP"] = np.ascontiguousarray(inputs["L0_attn_ln_gamma"].reshape(HC, 128).T, dtype=f32)
    d["betaP"] = np.ascontiguousarray(inputs["L0_attn_ln_beta"].reshape(HC, 128).T, dtype=f32)
    d["identP"] = np.eye(128, dtype=f32)
    o4 = np.zeros((128, 4), f32)
    o4[:, 0] = 1.0
    o4[0:64, 1] = 1.0
    o4[64:128, 2] = 1.0
    d["onesP"] = o4
    d["onesrP"] = np.ones((1, 128), f32)
    sel = np.zeros((NE, NE * 128), f32)
    for e in range(NE):
        sel[e, e * 128:(e + 1) * 128] = 1.0
    d["selP"] = sel
    d["jiotaP"] = np.broadcast_to(np.arange(384, dtype=f32)[None, :], (128, 384)).copy()
    d["piotaP"] = np.arange(128, dtype=f32).reshape(128, 1)
    xs = [np.ascontiguousarray(x[c].T, dtype=f32) for c in range(8)]
    return d, xs


def kernel(**inputs):
    shared, xs = _pack(inputs)
    nc = build(debug=False, n_phases=4)
    in_maps = [dict(shared, x_fm=xs[c]) for c in range(8)]
    res = run_bass_kernel_spmd(nc, in_maps, core_ids=list(range(8)))
    out = np.stack([res.results[c]["y"].T for c in range(8)])
    return np.ascontiguousarray(out, dtype=np.float32)
